# version 15
# baseline (speedup 1.0000x reference)
import numpy as np
import ml_dtypes
import concourse.bass as bass
import concourse.mybir as mybir
from concourse.bass_utils import run_bass_kernel_spmd

F32 = mybir.dt.float32
BF16 = mybir.dt.bfloat16
AF = mybir.ActivationFunctionType
ALU = mybir.AluOpType

D = 2048
T = 2048
H = 8192
JD = 16
KE = 16
KH = 64
TQ = 512
NQ = T // TQ
NS = 2
EPS = 1e-5
TMARKS = []


def _hadamard(n):
    i = np.arange(n)
    pc = np.vectorize(lambda v: bin(v).count("1"))(i[:, None] & i[None, :])
    return np.where(pc % 2 == 0, 1.0, -1.0).astype(np.float32)


class Sem:
    def __init__(self, nc, name):
        self.h = nc.alloc_semaphore(name)
        self.n = 0


def build_nc():
    nc = bass.Bass("TRN2", target_bir_lowering=False)

    xt = nc.dram_tensor("xt", [NS * D, T], F32, kind="ExternalInput").ap()
    wir = nc.dram_tensor("wir", [D, D], F32, kind="ExternalInput").ap()
    wfr = nc.dram_tensor("wfr", [D, D], F32, kind="ExternalInput").ap()
    wgr = nc.dram_tensor("wgr", [D, D], F32, kind="ExternalInput").ap()
    wor = nc.dram_tensor("wor", [D, D], F32, kind="ExternalInput").ap()
    w1r = nc.dram_tensor("w1r", [H, D], F32, kind="ExternalInput").ap()
    w3r = nc.dram_tensor("w3r", [H, D], F32, kind="ExternalInput").ap()
    w2t = nc.dram_tensor("w2t", [H, D], F32, kind="ExternalInput").ap()
    gnw = nc.dram_tensor("gnw", [D, 1], F32, kind="ExternalInput").ap()
    h128 = nc.dram_tensor("h128", [128, 128], BF16, kind="ExternalInput").ap()
    h64 = nc.dram_tensor("h64", [64, 64], BF16, kind="ExternalInput").ap()
    out = nc.dram_tensor("out", [NS * D, T], F32, kind="ExternalOutput").ap()
    w13b = nc.dram_tensor("w13b", [H, 2 * D], BF16, kind="Internal").ap()
    wifb = nc.dram_tensor("wifb", [D, 2 * D], BF16, kind="Internal").ap()
    wgb = nc.dram_tensor("wgb", [D, D], BF16, kind="Internal").ap()
    w2i = nc.dram_tensor("w2i", [H, D], BF16, kind="Internal").ap()
    w2p = nc.dram_tensor("w2p", [D, H], BF16, kind="Internal").ap()
    wob = nc.dram_tensor("wob", [D, D], BF16, kind="Internal").ap()
    xnd = nc.dram_tensor("xnd", [D, T], BF16, kind="Internal").ap()
    ogs = nc.dram_tensor("ogs", [D, T], BF16, kind="Internal").ap()

    R1 = nc.alloc_sbuf_tensor("R1", [128, JD * T], BF16)
    R2 = nc.alloc_sbuf_tensor("R2", [128, KE * T], BF16)
    sA = nc.alloc_sbuf_tensor("sA", [128, T], F32)
    sII = nc.alloc_sbuf_tensor("sII", [128, T], F32)
    sH = nc.alloc_sbuf_tensor("sH", [128, T], F32)
    sACC = nc.alloc_sbuf_tensor("sACC", [128, T], F32)
    xin2 = nc.alloc_sbuf_tensor("xin2", [128, 2 * T], F32)
    xinA = xin2[:, 0:T]
    xinB = xin2[:, T:2 * T]
    wbuf = nc.alloc_sbuf_tensor("wbuf", [128, 4 * T], BF16)
    rows = nc.alloc_sbuf_tensor("rows", [1, 3 * T], BF16)
    gnw_sb = nc.alloc_sbuf_tensor("gnw_sb", [128, KE], F32)
    h64_sb = nc.alloc_sbuf_tensor("h64_sb", [64, 64], BF16)
    h128_sb = wbuf[:, 0:128]
    ones_bf = nc.alloc_sbuf_tensor("ones_bf", [128, 1], BF16)
    ones_f = nc.alloc_sbuf_tensor("ones_f", [128, 1], F32)
    onesr_bf = nc.alloc_sbuf_tensor("onesr_bf", [1, 128], BF16)
    eps_c = nc.alloc_sbuf_tensor("eps_c", [1, 1], F32)

    PS = nc.alloc_psum_tensor("PS", [128, 4096], F32)

    def bank(i, w=512):
        return PS[:, i * 512:i * 512 + w]

    m_row = rows[:, 0:T]
    invs_row = rows[:, T:2 * T]
    nminvs_row = rows[:, 2 * T:3 * T]

    def xn_j(j):
        return R1[:, j * T:(j + 1) * T]

    def hidq_h(hc):
        return R1[:, hc * TQ:(hc + 1) * TQ]

    def xbf_slot(p):
        return R2[:, p * T:(p + 1) * T]

    def xsq_slot(p):
        return R2[:, (2 + p) * T:(3 + p) * T]

    def conv_slot(p):
        return R2[:, (8 + 4 * p) * T:(12 + 4 * p) * T]

    def hbf_k(k):
        return R2[:, k * T:(k + 1) * T]

    def xnq_slot():
        return R2[:, 0:JD * TQ]

    def ogq_slot():
        return R2[:, JD * TQ:2 * JD * TQ]

    def ring_slot(i):
        off = 2 * JD * TQ
        return R2[:, off + (i % 4) * 2 * T:off + ((i % 4) + 1) * 2 * T]

    NSLAB = 4

    def wslab(i):
        return wbuf[:, (i % NSLAB) * T:((i % NSLAB) + 1) * T]

    R2f = R2.bitcast(F32)

    def pbuf(i, jj):
        base = 4096 + (i % 2) * 4096
        return R2f[:, base + jj * T:base + (jj + 1) * T]

    def pbuf_full(i):
        base = 4096 + (i % 2) * 4096
        return R2f[:, base:base + 2 * T]

    def sg_slot(ph):
        return xinA[:, ph * TQ:(ph + 1) * TQ]

    def tmp_slot(ph):
        return xinB[:, ph * TQ:(ph + 1) * TQ]

    def outb_slot(pd):
        return xinA[:, (2 + pd) * TQ:(3 + pd) * TQ]

    sx = [Sem(nc, "sx0"), Sem(nc, "sx1")]
    sw = [Sem(nc, "sw0"), Sem(nc, "sw1")]
    sr = [Sem(nc, "sr0"), Sem(nc, "sr1")]
    sq = [Sem(nc, "sq0"), Sem(nc, "sq1")]
    sst = Sem(nc, "sst")
    sout = Sem(nc, "sout")
    spe = Sem(nc, "spe")
    sact = Sem(nc, "sact")
    sdve = Sem(nc, "sdve")
    sgp = Sem(nc, "sgp")

    PROG = {q: [] for q in ("sync", "gpsimd", "scalar", "vector", "tensor")}
    nmm = [0]

    def mark(label):
        TMARKS.append((label, nmm[0]))

    def E(q, fn, sem=None, k=1):
        if sem is not None:
            sem.n += k
        if q == "tensor":
            nmm[0] += 1

        def f(e, fn=fn, sem=sem, k=k):
            inst = fn(e)
            if sem is not None:
                inst.then_inc(sem.h, k)
        PROG[q].append(f)

    def W(q, sem, val=None):
        v = sem.n if val is None else val
        if v is None or v <= 0:
            return
        PROG[q].append(lambda e, s=sem.h, v=v: e.wait_ge(s, v))

    E("gpsimd", lambda e: e.memset(ones_bf[:], 1.0))
    E("gpsimd", lambda e: e.memset(ones_f[:], 1.0))
    E("gpsimd", lambda e: e.memset(onesr_bf[:], 1.0))
    E("gpsimd", lambda e: e.memset(eps_c[:], EPS), sem=sgp)
    memset_done = sgp.n
    E("sync", lambda e: e.dma_start(h128_sb[:], h128), sem=sst, k=16)
    E("sync", lambda e: e.dma_start(h64_sb[:], h64), sem=sst, k=16)
    for k in range(KE):
        E("sync", lambda e, k=k: e.dma_start(
            gnw_sb[:, k:k + 1], gnw[k * 128:(k + 1) * 128, :]), sem=sst, k=16)
    misc_done = sst.n

    mark("PhC")
    conv_jobs = []
    for i in range(16):
        conv_jobs.append((w1r[i * 512:(i + 1) * 512, :],
                          w13b[i * 512:(i + 1) * 512, 0:D]))
        conv_jobs.append((w3r[i * 512:(i + 1) * 512, :],
                          w13b[i * 512:(i + 1) * 512, D:2 * D]))
    for i in range(4):
        conv_jobs.append((wir[i * 512:(i + 1) * 512, :],
                          wifb[i * 512:(i + 1) * 512, 0:D]))
        conv_jobs.append((wfr[i * 512:(i + 1) * 512, :],
                          wifb[i * 512:(i + 1) * 512, D:2 * D]))
        conv_jobs.append((wgr[i * 512:(i + 1) * 512, :],
                          wgb[i * 512:(i + 1) * 512, :]))
        conv_jobs.append((wor[i * 512:(i + 1) * 512, :],
                          wob[i * 512:(i + 1) * 512, :]))
    conv_store = []
    for i, (src, dst) in enumerate(conv_jobs):
        p = i % 2
        sv = src.rearrange("(n p) d -> p n d", p=128)
        dv = dst.rearrange("(n p) d -> p n d", p=128)
        if i >= 2:
            W("gpsimd", sst, conv_store[i - 2])
        E("gpsimd", lambda e, sv=sv, p=p: e.dma_start(conv_slot(p), sv),
          sem=sw[p], k=16)
        W("sync", sw[p])
        E("sync", lambda e, dv=dv, p=p: e.dma_start(dv, conv_slot(p)),
          sem=sst, k=16)
        conv_store.append(sst.n)

    conv_all = sst.n
    W("tensor", sst, misc_done)
    W("tensor", sgp, memset_done)
    phw_act = {}
    phw_st = {}
    mark("PhW-A")

    def wa_in(p):
        return R2[:, p * 4 * T:(p + 1) * 4 * T]

    def wa_out(p):
        return R2[:, (8 + 4 * p) * T:(12 + 4 * p) * T]

    for g in range(16):
        p = g % 2
        if g >= 2:
            W("gpsimd", sact, phw_act[g - 2])
        E("gpsimd", lambda e, g=g, p=p: e.dma_start(
            wa_in(p), w2t[g * 512:(g + 1) * 512, :].rearrange(
                "(n p) d -> p n d", p=128)), sem=sw[p], k=16)
        W("tensor", sw[p])
        if g >= 1:
            W("tensor", sact, phw_act[g - 1])
        evac_c = {}
        for c in range(4):
            if c >= 2:
                W("tensor", sact, evac_c[c - 2])
            for t4 in range(4):
                E("tensor", lambda e, p=p, c=c, t4=t4: e.matmul(
                    PS[:, (c % 2) * 2048 + t4 * 512:
                       (c % 2) * 2048 + (t4 + 1) * 512],
                    h128_sb[:], wa_in(p)[:, c * 2048 + t4 * 512:
                                         c * 2048 + (t4 + 1) * 512],
                    start=True, stop=True),
                    sem=(spe if t4 == 3 else None))
            W("scalar", spe, spe.n)
            if g >= 2 and c == 0:
                W("scalar", sst, phw_st[g - 2])
            if g == 0 and c == 0:
                W("scalar", sst, conv_all)
            E("scalar", lambda e, p=p, c=c: e.activation(
                wa_out(p)[:, c * 2048:(c + 1) * 2048],
                PS[:, (c % 2) * 2048:((c % 2) + 1) * 2048], AF.Copy),
                sem=sact)
            evac_c[c] = sact.n
        phw_act[g] = sact.n
        W("sync", sact, phw_act[g])
        E("sync", lambda e, g=g, p=p: e.dma_start(
            w2i[g * 512:(g + 1) * 512, :].rearrange("(n p) d -> p n d",
                                                    p=128),
            wa_out(p).rearrange("q (n d) -> q n d", n=4)), sem=sst, k=16)
        phw_st[g] = sst.n
    passa_done = sst.n
    mark("PhW-B")

    def wb_in(p):
        return R2[0:64, p * 2 * T:(p + 1) * 2 * T]

    def wb_out(p):
        return R2[0:64, (4 + 2 * p) * T:(6 + 2 * p) * T]

    w2i_ab = w2i.rearrange("(a b) d -> a b d", b=128)
    w2p_st = w2p.rearrange("(dc p) (a dm) -> a p dc dm", p=128, dm=128)
    phw_act = {}
    phw_st = {}
    for g in range(64):
        b0 = 2 * g
        p = g % 2
        if g >= 2:
            W("gpsimd", sact, phw_act[g - 2])
        W("gpsimd", sst, passa_done)
        E("gpsimd", lambda e, b0=b0, p=p: e.dma_start(
            wb_in(p), w2i_ab[:, b0:b0 + 2, :]), sem=sw[p], k=16)
        W("tensor", sw[p])
        if g >= 1:
            W("tensor", sact, phw_act[g - 1])
        for bi in range(2):
            for t4 in range(4):
                E("tensor", lambda e, p=p, bi=bi, t4=t4: e.matmul(
                    PS[0:64, bi * 2048 + t4 * 512:bi * 2048 + (t4 + 1) * 512],
                    h64_sb[:], wb_in(p)[:, bi * 2048 + t4 * 512:
                                        bi * 2048 + (t4 + 1) * 512],
                    start=True, stop=True),
                    sem=(spe if t4 == 3 else None))
            W("scalar", spe, spe.n)
            if g >= 2 and bi == 0:
                W("scalar", sst, phw_st[g - 2])
            E("scalar", lambda e, p=p, bi=bi: e.activation(
                wb_out(p)[:, bi * 2048:(bi + 1) * 2048],
                PS[0:64, bi * 2048:(bi + 1) * 2048], AF.Copy), sem=sact)
        phw_act[g] = sact.n
        W("sync", sact, phw_act[g])
        for bi in range(2):
            E("sync", lambda e, b0=b0, p=p, bi=bi: e.dma_start(
                w2p_st[:, b0 + bi, :, :],
                wb_out(p)[:, bi * 2048:(bi + 1) * 2048].rearrange(
                    "a (dc dm) -> a dc dm", dm=128)), sem=sst, k=16)
        phw_st[g] = sst.n

    for q in PROG:
        W(q, sst, sst.n)
        W(q, sact, sact.n)
        W(q, spe, spe.n)
        W(q, sdve, sdve.n)
        W(q, sgp, sgp.n)

    slab_idx = [0]
    slab_ld = {}
    slab_consumed = {}

    def load_slab(key, src):
        i = slab_idx[0]
        assert i % 2 == 0
        slab_idx[0] += 2
        p = (i // 2) % 2
        for ii in (i, i + 1):
            if ii - NSLAB in slab_consumed:
                W("scalar", spe, slab_consumed[ii - NSLAB])
        E("scalar", lambda e, src=src, i=i: e.dma_start(
            wbuf[:, (i % NSLAB) * T:((i % NSLAB) + 2) * T], src),
            sem=sw[p], k=16)
        slab_ld[key] = (i, sw[p].n, p)
        return i

    xnd_v = xnd.rearrange("(j p) t -> p j t", p=128)
    ogs_v = ogs.rearrange("(k p) t -> p k t", p=128)

    pe_done = {}
    act_done = {}
    gp_done = {}
    dve_done = {}

    g_dve = {}
    for s in range(NS):
        xrow = s * D

        mark(f"s{s}:Ph1a")
        xt_pair = xt.rearrange("(r p) t -> r p t", p=128)
        for jp in range(JD // 2):
            pp_ = jp % 2
            if jp >= 2:
                W("sync", sact, act_done[("1a", 2 * jp - 3)])
                W("sync", sgp, gp_done[("1a", 2 * jp - 3)])
            E("sync", lambda e, jp=jp, xrow=xrow: e.dma_start(
                pbuf_full(jp), xt_pair[(xrow // 128) + 2 * jp:
                                       (xrow // 128) + 2 * jp + 2].rearrange(
                    "r p t -> p r t")), sem=sx[pp_], k=16)
            for jj in range(2):
                j = 2 * jp + jj
                p = j % 2
                xin = pbuf(jp, jj)
                W("scalar", sx[pp_])
                if j >= 2:
                    W("scalar", spe, pe_done[("1a", j - 2)])
                E("scalar", lambda e, p=p, xin=xin: e.activation(
                    xbf_slot(p), xin[:], AF.Copy), sem=sact)
                act_done[("1a", j)] = sact.n
                W("gpsimd", sx[pp_])
                if j >= 2:
                    W("gpsimd", spe, pe_done[("1a", j - 2)])
                E("gpsimd", lambda e, p=p, xin=xin: e.tensor_tensor(
                    xsq_slot(p), xin[:], xin[:], ALU.mult), sem=sgp)
                gp_done[("1a", j)] = sgp.n
                W("tensor", sact, act_done[("1a", j)])
                W("tensor", sgp, gp_done[("1a", j)])
                for t4 in range(4):
                    E("tensor", lambda e, j=j, p=p, t4=t4: e.matmul(
                        PS[0:1, t4 * 512:(t4 + 1) * 512], ones_bf[:],
                        xbf_slot(p)[:, t4 * 512:(t4 + 1) * 512],
                        start=(j == 0), stop=(j == JD - 1)))
                for t4 in range(4):
                    E("tensor", lambda e, j=j, p=p, t4=t4: e.matmul(
                        PS[0:1, 2048 + t4 * 512:2048 + (t4 + 1) * 512],
                        ones_bf[:],
                        xsq_slot(p)[:, t4 * 512:(t4 + 1) * 512],
                        start=(j == 0), stop=(j == JD - 1)),
                        sem=(spe if t4 == 3 else None))
                pe_done[("1a", j)] = spe.n

        mark(f"s{s}:stats")
        W("vector", spe, pe_done[("1a", JD - 1)])
        E("vector", lambda e: e.tensor_scalar(
            m_row, PS[0:1, 0:2048], 1.0 / D, None, ALU.mult), sem=sdve)
        E("vector", lambda e: e.tensor_tensor(
            nminvs_row, m_row, m_row, ALU.mult), sem=sdve)
        E("vector", lambda e: e.scalar_tensor_tensor(
            PS[0:1, 0:2048], PS[0:1, 2048:4096], 1.0 / D, nminvs_row,
            ALU.mult, ALU.subtract), sem=sdve)
        dve_var = sdve.n
        W("scalar", sdve, dve_var)
        W("scalar", sgp, memset_done)
        E("scalar", lambda e: e.activation(
            invs_row, PS[0:1, 0:2048], AF.Sqrt, bias=eps_c[:]), sem=sact)
        act_sd = sact.n
        W("vector", sact, act_sd)

        def _recip(e):
            with nc.allow_low_precision(reason="bf16 normalizer row"):
                return e.reciprocal(invs_row, invs_row)
        E("vector", _recip, sem=sdve)
        E("vector", lambda e: e.scalar_tensor_tensor(
            nminvs_row, m_row, -1.0, invs_row, ALU.mult, ALU.mult), sem=sdve)
        dve_rows = sdve.n
        W("tensor", sdve, dve_rows)
        for t4 in range(4):
            E("tensor", lambda e, t4=t4: e.matmul(
                bank(t4), onesr_bf[:], invs_row[:, t4 * 512:(t4 + 1) * 512],
                start=True, stop=True))
        for t4 in range(4):
            E("tensor", lambda e, t4=t4: e.matmul(
                bank(4 + t4), onesr_bf[:],
                nminvs_row[:, t4 * 512:(t4 + 1) * 512],
                start=True, stop=True), sem=(spe if t4 == 3 else None))
        pe_rep = spe.n
        W("vector", spe, pe_rep)
        E("vector", lambda e: e.tensor_copy(sA[:], PS[:, 0:2048]), sem=sdve)
        E("vector", lambda e: e.tensor_copy(sII[:], PS[:, 2048:4096]),
          sem=sdve)
        dve_rep = sdve.n

        mark(f"s{s}:Ph1b")
        xn_done = {}
        for jp in range(JD // 2):
            pp_ = jp % 2
            if jp <= 1:
                W("sync", sact, act_done[("1a", JD - 1)])
                W("sync", sgp, gp_done[("1a", JD - 1)])
            if jp >= 2:
                sm, cn = xn_done[2 * jp - 3]
                W("sync", sm, cn)
                sm, cn = xn_done[2 * jp - 4]
                W("sync", sm, cn)
            E("sync", lambda e, jp=jp, xrow=xrow: e.dma_start(
                pbuf_full(jp), xt_pair[(xrow // 128) + 2 * jp:
                                       (xrow // 128) + 2 * jp + 2].rearrange(
                    "r p t -> p r t")), sem=sx[pp_], k=16)
            for jj in range(2):
                j = 2 * jp + jj
                xin = pbuf(jp, jj)
                qn = "vector" if jj == 0 else "gpsimd"
                qsem = sdve if jj == 0 else sgp
                W(qn, sx[pp_])
                if j <= 1:
                    W(qn, sdve, dve_rep)
                E(qn, lambda e, xin=xin: e.tensor_tensor(
                    xin[:], xin[:], sA[:], ALU.mult))
                E(qn, lambda e, j=j, xin=xin: e.tensor_tensor(
                    xn_j(j), xin[:], sII[:], ALU.add), sem=qsem)
                xn_done[j] = (qsem, qsem.n)
                W("sync", qsem, qsem.n)
                E("sync", lambda e, j=j: e.dma_start(
                    xnd[j * 128:(j + 1) * 128, :], xn_j(j)), sem=sst, k=16)
        xn_stores = sst.n

        mark(f"s{s}:Ph2")
        W("gpsimd", sdve, dve_rep)
        E("gpsimd", lambda e: e.memset(sACC[:], 0.0), sem=sgp)
        acc_zero = sgp.n

        load_slab(("wif", s, 0), wifb[0:128, :])
        load_slab(("wif", s, 1), wifb[128:256, :])

        evac_act = {}
        evac_dve = {}
        scan_done = {}
        for k in range(KE):
            iw, iwc, pw = slab_ld[("wif", s, k)]
            fw = iw + 1
            for qt in range(4):
                hof = qt * 512
                b_i, b_f = 2 * qt, 2 * qt + 1
                if k == 0 and qt == 0:
                    W("tensor", sdve, dve_rep)
                if (k - 1, qt) in evac_dve:
                    W("tensor", sdve, evac_dve[(k - 1, qt)])
                    W("tensor", sact, evac_act[(k - 1, qt)])
                if qt == 0:
                    W("tensor", sw[pw], iwc)
                for j in range(JD):
                    if k == 0 and qt == 0:
                        W("tensor", xn_done[j][0], xn_done[j][1])
                    E("tensor", lambda e, j=j, b_i=b_i, iw=iw, hof=hof:
                      e.matmul(
                        bank(b_i), wslab(iw)[:, j * 128:(j + 1) * 128],
                        xn_j(j)[:, hof:hof + 512],
                        start=(j == 0), stop=(j == JD - 1)))
                    E("tensor", lambda e, j=j, b_f=b_f, fw=fw, hof=hof:
                      e.matmul(
                        bank(b_f), wslab(fw)[:, j * 128:(j + 1) * 128],
                        xn_j(j)[:, hof:hof + 512],
                        start=(j == 0), stop=(j == JD - 1)),
                        sem=(spe if j == JD - 1 else None))
                pe_done[("2", k, qt)] = spe.n
                W("scalar", spe, pe_done[("2", k, qt)])
                if qt == 0 and k > 0:
                    W("scalar", sdve, scan_done[k - 1])
                E("scalar", lambda e, b_i=b_i, hof=hof: e.activation(
                    xinB[:, hof:hof + 512], bank(b_i), AF.Sigmoid))
                E("scalar", lambda e, b_f=b_f, hof=hof: e.activation(
                    sA[:, hof:hof + 512], bank(b_f), AF.Sigmoid), sem=sact)
                evac_act[(k, qt)] = sact.n
                W("vector", sact, evac_act[(k, qt)])
                E("vector", lambda e, hof=hof: e.tensor_scalar(
                    xinA[:, hof:hof + 512], sA[:, hof:hof + 512], -1.0, 1.0,
                    ALU.mult, ALU.add))
                E("vector", lambda e, b_i=b_i, hof=hof: e.tensor_tensor(
                    xinB[:, hof:hof + 512], bank(b_i),
                    xinB[:, hof:hof + 512], ALU.mult))
                E("vector", lambda e, hof=hof: e.tensor_tensor(
                    sII[:, hof:hof + 512], xinB[:, hof:hof + 512],
                    xinA[:, hof:hof + 512], ALU.mult), sem=sdve)
                evac_dve[(k, qt)] = sdve.n
            slab_consumed[iw] = pe_done[("2", k, 3)]
            slab_consumed[fw] = pe_done[("2", k, 3)]
            if k + 2 < KE:
                load_slab(("wif", s, k + 2),
                          wifb[(k + 2) * 128:(k + 3) * 128, :])
            E("vector", lambda e, k=k: e.tensor_tensor_scan(
                hbf_k(k), sA[:], sII[:], 0.0, ALU.mult, ALU.add), sem=sdve)
            scan_done[k] = sdve.n
            W("gpsimd", sdve, scan_done[k])
            if k == 0:
                W("gpsimd", sgp, acc_zero)
            E("gpsimd", lambda e, k=k: e.tensor_tensor(
                sH[:], hbf_k(k), hbf_k(k), ALU.mult))
            E("gpsimd", lambda e: e.tensor_tensor(
                sACC[:], sACC[:], sH[:], ALU.add), sem=sgp)
            gp_done[("2", k)] = sgp.n

        mark(f"s{s}:rms")
        W("tensor", sgp, gp_done[("2", KE - 1)])
        W("tensor", sdve, evac_dve[(KE - 1, 1)])
        W("tensor", sact, evac_act[(KE - 1, 1)])
        for t4 in range(4):
            E("tensor", lambda e, t4=t4: e.matmul(
                PS[0:1, t4 * 512:(t4 + 1) * 512], ones_f[:],
                sACC[:, t4 * 512:(t4 + 1) * 512],
                start=True, stop=True), sem=(spe if t4 == 3 else None))
        pe_ssq = spe.n
        W("scalar", spe, pe_ssq)
        E("scalar", lambda e: e.activation(
            invs_row, PS[0:1, 0:2048], AF.Sqrt, bias=eps_c[:],
            scale=1.0 / D), sem=sact)
        act_sd2 = sact.n
        W("vector", sact, act_sd2)

        def _recip2(e):
            with nc.allow_low_precision(reason="bf16 normalizer row"):
                return e.reciprocal(invs_row, invs_row)
        E("vector", _recip2, sem=sdve)
        dve_rstd = sdve.n
        W("tensor", sdve, dve_rstd)
        W("tensor", sdve, evac_dve[(KE - 1, 3)])
        W("tensor", sact, evac_act[(KE - 1, 3)])
        for t4 in range(4):
            E("tensor", lambda e, t4=t4: e.matmul(
                bank(4 + t4), onesr_bf[:],
                invs_row[:, t4 * 512:(t4 + 1) * 512],
                start=True, stop=True), sem=(spe if t4 == 3 else None))
        pe_rstd = spe.n
        W("vector", spe, pe_rstd)
        W("vector", sgp, gp_done[("2", KE - 1)])
        E("vector", lambda e: e.tensor_copy(sACC[:], PS[:, 2048:4096]),
          sem=sdve)
        dve_rstd_rep = sdve.n

        mark(f"s{s}:Ph3")
        load_slab(("wg", s, 0),
                  wgb[0:256, :].rearrange("(n p) d -> p n d", p=128))
        og_store = {}
        for k in range(KE):
            gw0, gwc, pg = slab_ld[("wg", s, k - k % 2)]
            gw = gw0 + k % 2
            base = 0 if k % 2 == 0 else 4
            if k % 2 == 0:
                W("tensor", sw[pg], gwc)
            if k == 0:
                W("tensor", sact, act_sd2)
            if k == 1:
                W("tensor", sdve, dve_rstd_rep)
            if k >= 2:
                W("tensor", sdve, g_dve[(s, k - 2)])
            for j in range(JD):
                for t4 in range(4):
                    E("tensor", lambda e, j=j, base=base, gw=gw, t4=t4:
                      e.matmul(
                        bank(base + t4),
                        wslab(gw)[:, j * 128:(j + 1) * 128],
                        xn_j(j)[:, t4 * 512:(t4 + 1) * 512],
                        start=(j == 0), stop=(j == JD - 1)),
                        sem=(spe if (j == JD - 1 and t4 == 3) else None))
            pe_done[("3", k)] = spe.n
            slab_consumed[gw] = pe_done[("3", k)]
            if k % 2 == 0 and k + 2 < KE:
                load_slab(("wg", s, k + 2),
                          wgb[(k + 2) * 128:(k + 4) * 128, :].rearrange(
                              "(n p) d -> p n d", p=128))
            W("scalar", spe, pe_done[("3", k)])
            E("scalar", lambda e, base=base: e.activation(
                xinB[:], PS[:, base * 512:(base + 4) * 512], AF.Sigmoid),
                sem=sact)
            act_done[("3", k)] = sact.n
            W("vector", sact, act_done[("3", k)])
            E("vector", lambda e, base=base: e.tensor_tensor(
                xinA[:], PS[:, base * 512:(base + 4) * 512], xinB[:],
                ALU.mult))
            E("vector", lambda e, k=k: e.scalar_tensor_tensor(
                xinB[:], xinA[:], gnw_sb[:, k:k + 1], sACC[:], ALU.mult,
                ALU.mult))
            E("vector", lambda e, k=k: e.tensor_tensor(
                hbf_k(k), xinB[:], hbf_k(k), ALU.mult), sem=sdve)
            g_dve[(s, k)] = sdve.n
            W("sync", sdve, g_dve[(s, k)])
            E("sync", lambda e, k=k: e.dma_start(
                ogs[k * 128:(k + 1) * 128, :], hbf_k(k)), sem=sst, k=16)
            og_store[k] = sst.n
        og_all = sst.n

        ring_pe = {}
        ring_i = [0]
        q_xnq_done = {}
        hid_dve = {}
        out_dve = {}
        outst = {}
        pe_4b_last = {}
        xnq_lds = {}
        for q in range(NQ):
            pq = q % 2
            if q == 0:
                W("sync", sst, og_store[3])
                W("sync", sst, xn_stores)
                E("sync", lambda e: e.dma_start(
                    xnq_slot(), xnd_v[:, :, 0:TQ]), sem=sq[0], k=16)
                xnq_lds[0] = sq[0].n
            xnq_ld = xnq_lds[q]
            if q > 0:
                W("sync", spe, pe_4b_last[q - 1])
            W("sync", sst, og_all)
            E("sync", lambda e, q=q: e.dma_start(
                ogq_slot(), ogs_v[:, :, q * TQ:(q + 1) * TQ]),
                sem=sq[1], k=16)
            ogq_ld = sq[1].n

            mark(f"s{s}:Ph4a{q}")
            load_slab(("w13", s, q, 0), w13b[0:128, :])
            load_slab(("w13", s, q, 1), w13b[128:256, :])
            for hc in range(KH):
                ph = hc % 2
                b_ug, b_ul = (0, 1) if ph == 0 else (2, 3)
                i1, c1, p1 = slab_ld[("w13", s, q, hc)]
                i3 = i1 + 1
                W("tensor", sw[p1], c1)
                if hc == 0:
                    W("tensor", sq[0], xnq_ld)
                    if q == 0:
                        W("tensor", sdve, g_dve[(s, KE - 1)])
                if (q, hc - 2) in hid_dve:
                    W("tensor", sdve, hid_dve[(q, hc - 2)])
                for j in range(JD):
                    E("tensor", lambda e, j=j, b_ug=b_ug, i1=i1: e.matmul(
                        bank(b_ug), wslab(i1)[:, j * 128:(j + 1) * 128],
                        xnq_slot()[:, j * TQ:(j + 1) * TQ],
                        start=(j == 0), stop=(j == JD - 1)))
                    E("tensor", lambda e, j=j, b_ul=b_ul, i3=i3: e.matmul(
                        bank(b_ul), wslab(i3)[:, j * 128:(j + 1) * 128],
                        xnq_slot()[:, j * TQ:(j + 1) * TQ],
                        start=(j == 0), stop=(j == JD - 1)),
                        sem=(spe if j == JD - 1 else None))
                pe_done[("4a", q, hc)] = spe.n
                slab_consumed[i1] = spe.n
                slab_consumed[i3] = spe.n
                if hc + 2 < KH:
                    load_slab(("w13", s, q, hc + 2),
                              w13b[(hc + 2) * 128:(hc + 3) * 128, :])
                W("scalar", spe, pe_done[("4a", q, hc)])
                E("scalar", lambda e, b_ug=b_ug, ph=ph: e.activation(
                    sg_slot(ph), bank(b_ug), AF.Sigmoid), sem=sact)
                act_done[("4a", q, hc)] = sact.n
                W("vector", sact, act_done[("4a", q, hc)])
                E("vector", lambda e, b_ug=b_ug, ph=ph: e.tensor_tensor(
                    tmp_slot(ph), bank(b_ug), sg_slot(ph), ALU.mult))
                E("vector", lambda e, b_ul=b_ul, ph=ph, hc=hc:
                  e.tensor_tensor(
                    hidq_h(hc), tmp_slot(ph), bank(b_ul), ALU.mult),
                    sem=sdve)
                hid_dve[(q, hc)] = sdve.n
            q_xnq_done[q] = pe_done[("4a", q, KH - 1)]

            mark(f"s{s}:Ph4b{q}")
            for dc in range(JD):
                pd = dc % 2
                b_out = 6 + pd
                key = (q, dc - 2) if dc >= 2 else (q - 1, JD - 2 + dc)
                if key in outst:
                    W("gpsimd", sout, outst[key])
                if q == 0 and dc < 2:
                    W("gpsimd", sdve, g_dve[(s, KE - 1)])
                E("gpsimd", lambda e, dc=dc, q=q, pd=pd, xrow=xrow:
                  e.dma_start(
                    outb_slot(pd),
                    xt[xrow + dc * 128:xrow + (dc + 1) * 128,
                       q * TQ:(q + 1) * TQ]), sem=sx[pd], k=16)
                xres_ld = sx[pd].n
                if dc == 0:
                    W("tensor", sdve, hid_dve[(q, KH - 1)])
                    W("tensor", sq[1], ogq_ld)
                    if q == 0:
                        W("sync", sst, og_all)
                    if q + 1 < NQ:
                        W("sync", spe, q_xnq_done[q])
                        E("sync", lambda e, q=q: e.dma_start(
                            xnq_slot(),
                            xnd_v[:, :, (q + 1) * TQ:(q + 2) * TQ]),
                            sem=sq[0], k=16)
                        xnq_lds[q + 1] = sq[0].n
                if (q, dc - 2) in out_dve:
                    W("tensor", sdve, out_dve[(q, dc - 2)])
                elif (q - 1, JD - 2 + dc) in out_dve:
                    W("tensor", sdve, out_dve[(q - 1, JD - 2 + dc)])
                for gi in range(3):
                    ri = ring_i[0]
                    ring_i[0] += 1
                    if ri - 4 in ring_pe:
                        W("sync", spe, ring_pe[ri - 4])
                    if gi < 2:
                        E("sync", lambda e, dc=dc, gi=gi, ri=ri: e.dma_start(
                            ring_slot(ri),
                            w2p[dc * 128:(dc + 1) * 128,
                                gi * 4096:(gi + 1) * 4096]),
                            sem=sr[ri % 2], k=16)
                    else:
                        E("sync", lambda e, dc=dc, ri=ri: e.dma_start(
                            ring_slot(ri)[:, 0:2048],
                            wob[dc * 128:(dc + 1) * 128, :]),
                            sem=sr[ri % 2], k=16)
                    rc = sr[ri % 2].n
                    W("tensor", sr[ri % 2], rc)
                    nt = 32 if gi < 2 else 16
                    for ti in range(nt):
                        if gi < 2:
                            hc = gi * 32 + ti
                            E("tensor", lambda e, b_out=b_out, ri=ri, ti=ti,
                              hc=hc: e.matmul(
                                bank(b_out),
                                ring_slot(ri)[:, ti * 128:(ti + 1) * 128],
                                hidq_h(hc),
                                start=(hc == 0), stop=False),
                                sem=(spe if ti == nt - 1 else None))
                        else:
                            E("tensor", lambda e, b_out=b_out, ri=ri, ti=ti:
                              e.matmul(
                                bank(b_out),
                                ring_slot(ri)[:, ti * 128:(ti + 1) * 128],
                                ogq_slot()[:, ti * TQ:(ti + 1) * TQ],
                                start=False, stop=(ti == 15)),
                                sem=(spe if ti == 15 else None))
                    ring_pe[ri] = spe.n
                pe_out = spe.n
                pe_4b_last[q] = pe_out
                W("vector", spe, pe_out)
                W("vector", sx[pd], xres_ld)
                E("vector", lambda e, b_out=b_out, pd=pd: e.tensor_tensor(
                    outb_slot(pd), bank(b_out), outb_slot(pd), ALU.add),
                    sem=sdve)
                out_dve[(q, dc)] = sdve.n
                W("gpsimd", sdve, out_dve[(q, dc)])
                E("gpsimd", lambda e, dc=dc, q=q, pd=pd, xrow=xrow:
                  e.dma_start(
                    out[xrow + dc * 128:xrow + (dc + 1) * 128,
                        q * TQ:(q + 1) * TQ], outb_slot(pd)),
                    sem=sout, k=16)
                outst[(q, dc)] = sout.n

        for qn in PROG:
            W(qn, spe, spe.n)
            W(qn, sact, sact.n)
            W(qn, sdve, sdve.n)
            W(qn, sgp, sgp.n)
            W(qn, sout, sout.n)

    W("sync", sout, sout.n)

    with nc.Block() as block:
        @block.sync
        def _(eng):
            for f in PROG["sync"]:
                f(eng)

        @block.gpsimd
        def _(eng):
            for f in PROG["gpsimd"]:
                f(eng)

        @block.scalar
        def _(eng):
            for f in PROG["scalar"]:
                f(eng)

        @block.vector
        def _(eng):
            for f in PROG["vector"]:
                f(eng)

        @block.tensor
        def _(eng):
            for f in PROG["tensor"]:
                f(eng)

    return nc


def kernel(L0_attn_hidden_states, L0_attn_Wi, L0_attn_Wf, L0_attn_Wg,
           L0_attn_Wo, L0_attn_g_norm_weight, L0_ffn_W1, L0_ffn_W2,
           L0_ffn_W3):
    x = np.asarray(L0_attn_hidden_states, dtype=np.float32)
    Wi = np.asarray(L0_attn_Wi, dtype=np.float32)
    Wf = np.asarray(L0_attn_Wf, dtype=np.float32)
    Wg = np.asarray(L0_attn_Wg, dtype=np.float32)
    Wo = np.asarray(L0_attn_Wo, dtype=np.float32)
    gnwv = np.asarray(L0_attn_g_norm_weight, dtype=np.float32)
    W1 = np.asarray(L0_ffn_W1, dtype=np.float32)
    W2 = np.asarray(L0_ffn_W2, dtype=np.float32)
    W3 = np.asarray(L0_ffn_W3, dtype=np.float32)

    xt_all = np.ascontiguousarray(x.transpose(0, 2, 1))

    def tile_de(W):
        return np.ascontiguousarray(
            W.reshape(16, 128, 16, 128).transpose(2, 1, 0, 3).reshape(
                2048, 2048))

    def tile_hd(W):
        return np.ascontiguousarray(
            W.reshape(64, 128, 16, 128).transpose(0, 3, 2, 1).reshape(
                8192, 2048))

    wir = tile_de(Wi)
    wfr = tile_de(Wf)
    wgr = tile_de(Wg)
    wor = tile_de(Wo)
    w1r = tile_hd(W1)
    w3r = tile_hd(W3)
    w2tt = np.ascontiguousarray(W2.T)
    h128v = np.ascontiguousarray(
        (_hadamard(128) * (2.0 ** -7)).astype(ml_dtypes.bfloat16))
    h64v = np.ascontiguousarray(
        (_hadamard(64) * np.sqrt(2.0)).astype(ml_dtypes.bfloat16))
    gnw2 = np.ascontiguousarray(gnwv.reshape(2048, 1))

    nc = build_nc()
    in_maps = []
    for c in range(8):
        in_maps.append({
            "xt": np.ascontiguousarray(
                xt_all[2 * c:2 * c + 2].reshape(NS * D, T)),
            "wir": wir, "wfr": wfr, "wgr": wgr, "wor": wor,
            "w1r": w1r, "w3r": w3r, "w2t": w2tt,
            "gnw": gnw2, "h128": h128v, "h64": h64v,
        })
    res = run_bass_kernel_spmd(nc, in_maps, core_ids=list(range(8)))
    outs = [res.results[c]["out"].reshape(NS, D, T) for c in range(8)]
    full = np.concatenate(outs, axis=0)
    return np.ascontiguousarray(full.transpose(0, 2, 1))


# version 22
# speedup vs baseline: 1.0869x; 1.0869x over previous
import numpy as np
import ml_dtypes
import concourse.bass as bass
import concourse.mybir as mybir
from concourse.bass_utils import run_bass_kernel_spmd

F32 = mybir.dt.float32
BF16 = mybir.dt.bfloat16
AF = mybir.ActivationFunctionType
ALU = mybir.AluOpType

D = 2048
T = 2048
H = 8192
JD = 16
KE = 16
KH = 64
TQ = 512
NQ = T // TQ
NS = 2
EPS = 1e-5
TMARKS = []


def _hadamard(n):
    i = np.arange(n)
    pc = np.vectorize(lambda v: bin(v).count("1"))(i[:, None] & i[None, :])
    return np.where(pc % 2 == 0, 1.0, -1.0).astype(np.float32)


class Sem:
    def __init__(self, nc, name):
        self.h = nc.alloc_semaphore(name)
        self.n = 0


def build_nc():
    nc = bass.Bass("TRN2", target_bir_lowering=False)

    xt = nc.dram_tensor("xt", [NS * D, T], F32, kind="ExternalInput").ap()
    wir = nc.dram_tensor("wir", [D, D], F32, kind="ExternalInput").ap()
    wfr = nc.dram_tensor("wfr", [D, D], F32, kind="ExternalInput").ap()
    wgr = nc.dram_tensor("wgr", [D, D], F32, kind="ExternalInput").ap()
    wor = nc.dram_tensor("wor", [D, D], F32, kind="ExternalInput").ap()
    w1r = nc.dram_tensor("w1r", [H, D], F32, kind="ExternalInput").ap()
    w3r = nc.dram_tensor("w3r", [H, D], F32, kind="ExternalInput").ap()
    w2t = nc.dram_tensor("w2t", [H, D], F32, kind="ExternalInput").ap()
    gnw = nc.dram_tensor("gnw", [D, 1], F32, kind="ExternalInput").ap()
    h128 = nc.dram_tensor("h128", [128, 128], BF16, kind="ExternalInput").ap()
    h64 = nc.dram_tensor("h64", [64, 64], BF16, kind="ExternalInput").ap()
    out = nc.dram_tensor("out", [NS * D, T], F32, kind="ExternalOutput").ap()
    w13b = nc.dram_tensor("w13b", [H, 2 * D], BF16, kind="Internal").ap()
    wifb = nc.dram_tensor("wifb", [D, 2 * D], BF16, kind="Internal").ap()
    wgb = nc.dram_tensor("wgb", [D, D], BF16, kind="Internal").ap()
    w2i = nc.dram_tensor("w2i", [H, D], BF16, kind="Internal").ap()
    w2p = nc.dram_tensor("w2p", [D, H], BF16, kind="Internal").ap()
    wob = nc.dram_tensor("wob", [D, D], BF16, kind="Internal").ap()
    xnd = nc.dram_tensor("xnd", [D, T], BF16, kind="Internal").ap()
    ogs = nc.dram_tensor("ogs", [D, T], BF16, kind="Internal").ap()
    hs = nc.dram_tensor("hs", [D, T], BF16, kind="Internal").ap()

    R1 = nc.alloc_sbuf_tensor("R1", [128, JD * T], BF16)
    R2 = nc.alloc_sbuf_tensor("R2", [128, KE * T], BF16)
    sA = nc.alloc_sbuf_tensor("sA", [128, T], F32)
    sII = nc.alloc_sbuf_tensor("sII", [128, T], F32)
    sH = nc.alloc_sbuf_tensor("sH", [128, T], F32)
    sACC = nc.alloc_sbuf_tensor("sACC", [128, T], F32)
    xin2 = nc.alloc_sbuf_tensor("xin2", [128, 2 * T], F32)
    xinA = xin2[:, 0:T]
    xinB = xin2[:, T:2 * T]
    wbuf = nc.alloc_sbuf_tensor("wbuf", [128, 4 * T], BF16)
    rows = nc.alloc_sbuf_tensor("rows", [1, 3 * T], BF16)
    gnw_sb = nc.alloc_sbuf_tensor("gnw_sb", [128, KE], F32)
    h64_sb = nc.alloc_sbuf_tensor("h64_sb", [64, 64], BF16)
    h128_sb = nc.alloc_sbuf_tensor("h128_sb", [128, 128], BF16)
    ones_bf = nc.alloc_sbuf_tensor("ones_bf", [128, 1], BF16)
    ones_f = nc.alloc_sbuf_tensor("ones_f", [128, 1], F32)
    onesr_bf = nc.alloc_sbuf_tensor("onesr_bf", [1, 128], BF16)
    eps_c = nc.alloc_sbuf_tensor("eps_c", [1, 1], F32)

    PS = nc.alloc_psum_tensor("PS", [128, 4096], F32)

    def bank(i, w=512):
        return PS[:, i * 512:i * 512 + w]

    m_row = rows[:, 0:T]
    invs_row = rows[:, T:2 * T]
    nminvs_row = rows[:, 2 * T:3 * T]

    def xn_j(j):
        return R1[:, j * T:(j + 1) * T]

    def hidq_h(hc):
        return R1[:, hc * TQ:(hc + 1) * TQ]

    def xbf_slot(p):
        return R2[:, p * T:(p + 1) * T]

    def xsq_slot(p):
        return R2[:, (2 + p) * T:(3 + p) * T]

    def conv_slot(p):
        return R2[:, (8 + 4 * p) * T:(12 + 4 * p) * T]

    def hsb(k):
        return R2[:, (k % 2) * T:((k % 2) + 1) * T]

    def ogo(k):
        return R2[:, (2 + k % 2) * T:(3 + k % 2) * T]

    def xnq_slot():
        return R2[:, 0:JD * TQ]

    def ogq_slot():
        return R2[:, JD * TQ:2 * JD * TQ]

    def ring_slot(i):
        off = 2 * JD * TQ
        return R2[:, off + (i % 4) * 2 * T:off + ((i % 4) + 1) * 2 * T]

    NSLAB = 4

    def wslab(i):
        return wbuf[:, (i % NSLAB) * T:((i % NSLAB) + 1) * T]

    R2f = R2.bitcast(F32)

    def pbuf(i, jj):
        base = 4096 + (i % 2) * 4096
        return R2f[:, base + jj * T:base + (jj + 1) * T]

    def pbuf_full(i):
        base = 4096 + (i % 2) * 4096
        return R2f[:, base:base + 2 * T]

    def sg_slot(ph):
        return xinA[:, ph * TQ:(ph + 1) * TQ]

    def tmp_slot(ph):
        return xinB[:, ph * TQ:(ph + 1) * TQ]

    def outb_slot(pd):
        return xinA[:, (2 + pd) * TQ:(3 + pd) * TQ]

    sx = [Sem(nc, "sx0"), Sem(nc, "sx1")]
    sw = [Sem(nc, "sw0"), Sem(nc, "sw1")]
    sr = [Sem(nc, "sr0"), Sem(nc, "sr1")]
    sq = [Sem(nc, "sq0"), Sem(nc, "sq1")]
    sst = Sem(nc, "sst")
    sxn = Sem(nc, "sxn")
    sout = Sem(nc, "sout")
    spe = Sem(nc, "spe")
    sact = Sem(nc, "sact")
    sdve = Sem(nc, "sdve")
    sgp = Sem(nc, "sgp")

    PROG = {q: [] for q in ("sync", "gpsimd", "scalar", "vector", "tensor")}
    nmm = [0]

    def mark(label):
        TMARKS.append((label, nmm[0]))

    def E(q, fn, sem=None, k=1):
        if sem is not None:
            sem.n += k
        if q == "tensor":
            nmm[0] += 1

        def f(e, fn=fn, sem=sem, k=k):
            inst = fn(e)
            if sem is not None:
                inst.then_inc(sem.h, k)
        PROG[q].append(f)

    def W(q, sem, val=None):
        v = sem.n if val is None else val
        if v is None or v <= 0:
            return
        PROG[q].append(lambda e, s=sem.h, v=v: e.wait_ge(s, v))

    E("gpsimd", lambda e: e.memset(ones_bf[:], 1.0))
    E("gpsimd", lambda e: e.memset(ones_f[:], 1.0))
    E("gpsimd", lambda e: e.memset(onesr_bf[:], 1.0))
    E("gpsimd", lambda e: e.memset(eps_c[:], EPS), sem=sgp)
    memset_done = sgp.n
    E("sync", lambda e: e.dma_start(h128_sb[:], h128), sem=sst, k=16)
    E("sync", lambda e: e.dma_start(h64_sb[:], h64), sem=sst, k=16)
    for k in range(KE):
        E("sync", lambda e, k=k: e.dma_start(
            gnw_sb[:, k:k + 1], gnw[k * 128:(k + 1) * 128, :]), sem=sst, k=16)
    misc_done = sst.n

    scv = [Sem(nc, "scv0"), Sem(nc, "scv1")]
    sgst = Sem(nc, "sgst")
    saw = [Sem(nc, "saw0"), Sem(nc, "saw1")]

    conv_jobs = []
    for i in range(8):
        conv_jobs.append((wir[i * 256:(i + 1) * 256, :],
                          wifb[i * 256:(i + 1) * 256, 0:D]))
        conv_jobs.append((wfr[i * 256:(i + 1) * 256, :],
                          wifb[i * 256:(i + 1) * 256, D:2 * D]))
    CONV_WG0 = len(conv_jobs)
    for i in range(8):
        conv_jobs.append((wgr[i * 256:(i + 1) * 256, :],
                          wgb[i * 256:(i + 1) * 256, :]))
    CONV_W13 = len(conv_jobs)
    for i in range(32):
        conv_jobs.append((w1r[i * 256:(i + 1) * 256, :],
                          w13b[i * 256:(i + 1) * 256, 0:D]))
        conv_jobs.append((w3r[i * 256:(i + 1) * 256, :],
                          w13b[i * 256:(i + 1) * 256, D:2 * D]))
    CONV_WO = len(conv_jobs)
    for i in range(8):
        conv_jobs.append((wor[i * 256:(i + 1) * 256, :],
                          wob[i * 256:(i + 1) * 256, :]))
    conv_idx = [0]
    conv_store = {}

    def conv_slot2(i):
        return R2[:, (12 + 2 * (i % 2)) * T:(14 + 2 * (i % 2)) * T]

    def conv_emit(n):
        for _ in range(n):
            i = conv_idx[0]
            if i >= len(conv_jobs):
                return
            conv_idx[0] += 1
            src_, dst_ = conv_jobs[i]
            sv = src_.rearrange("(n p) d -> p n d", p=128)
            dv = dst_.rearrange("(n p) d -> p n d", p=128)
            p = i % 2
            if i >= 2:
                W("gpsimd", sgst, conv_store[i - 2])
            E("gpsimd", lambda e, sv=sv, p=p, i=i: e.dma_start(
                conv_slot2(p), sv), sem=scv[p], k=16)
            W("gpsimd", scv[p])
            E("gpsimd", lambda e, dv=dv, p=p, i=i: e.dma_start(
                dv, conv_slot2(p)), sem=sgst, k=16)
            conv_store[i] = sgst.n

    def conv_gate(q, i):
        if i in conv_store:
            W(q, sgst, conv_store[i])
        else:
            raise RuntimeError(f"conv job {i} not yet emitted")

    aw_ld = {}
    aw_pe = {}
    aw_evac = {}
    aw_st = {}

    def a_in(p):
        return R2[:, (4 + 2 * p) * T:(6 + 2 * p) * T]

    def a_out(n):
        return R2[:, 8 * T + (n % 4) * 1024:8 * T + (n % 4) * 1024 + 1024]

    def a_emit_load(g, xn_done):
        p = g % 2
        if g < 2:
            sm, cn = xn_done[13]
            W("gpsimd", sm, cn)
            sm, cn = xn_done[12]
            W("gpsimd", sm, cn)
        else:
            W("gpsimd", spe, aw_pe[(g - 2) * 4 + 3])
        E("gpsimd", lambda e, g=g, p=p: e.dma_start(
            a_in(p), w2t[g * 256:(g + 1) * 256, :].rearrange(
                "(n p) d -> p n d", p=128)), sem=saw[p], k=16)
        aw_ld[g] = saw[p].n

    def a_emit_round(n, xn_done):
        a = n // 2
        r = n % 2
        g = a // 2
        p = g % 2
        col = (a % 2) * 2048 + r * 1024
        if n % 4 == 0:
            W("tensor", saw[p], aw_ld[g])
        if n >= 1:
            W("tensor", sact, aw_evac[n - 1])
        for t2 in range(2):
            E("tensor", lambda e, p=p, col=col, t2=t2: e.matmul(
                bank(6 + t2), h128_sb[:],
                a_in(p)[:, col + t2 * 512:col + (t2 + 1) * 512],
                start=True, stop=True),
                sem=(spe if t2 == 1 else None))
        aw_pe[n] = spe.n
        W("scalar", spe, aw_pe[n])
        if n >= 4:
            W("scalar", sst, aw_st[n - 4])
        elif n == 0:
            sm, cn = xn_done[15]
            W("scalar", sm, cn)
            sm, cn = xn_done[14]
            W("scalar", sm, cn)
        E("scalar", lambda e, n=n: e.activation(
            a_out(n), PS[:, 6 * 512:8 * 512], AF.Copy), sem=sact)
        aw_evac[n] = sact.n
        W("sync", sact, aw_evac[n])
        E("sync", lambda e, a=a, r=r, n=n: e.dma_start(
            w2i[a * 128:(a + 1) * 128, r * 1024:(r + 1) * 1024], a_out(n)),
            sem=sst, k=16)
        aw_st[n] = sst.n
        if n % 4 == 3 and g + 2 < 32:
            a_emit_load(g + 2, xn_done)

    w2i_ab = w2i.rearrange("(a b) d -> a b d", b=128)
    w2p_st = w2p.rearrange("(dc p) (a dm) -> a p dc dm", p=128, dm=128)

    def b_in(p):
        return R2[0:64, (4 + 4 * p) * T:(8 + 4 * p) * T]

    def bw_out(p):
        return R2[0:64, (14 + p) * T:(15 + p) * T]

    def b_emit_block(ph3_dve, ph3_act):
        b_ld = {}
        b_pe = {}
        b_evac = {}
        b_st = {}

        def b_load(g):
            p = g % 2
            if g < 2:
                W("sync", spe, aw_pe[127])
                W("sync", sst, aw_st[127])
                W("sync", sdve, ph3_dve)
            else:
                W("sync", spe, b_pe[4 * (g - 2) + 3])
            E("sync", lambda e, g=g, p=p: e.dma_start(
                b_in(p), w2i_ab[:, 4 * g:4 * g + 4, :]), sem=sr[p], k=16)
            b_ld[g] = sr[p].n

        b_load(0)
        b_load(1)
        for g in range(32):
            p = g % 2
            for bi in range(4):
                b = 4 * g + bi
                pb = b % 2
                if bi == 0:
                    W("tensor", sr[p], b_ld[g])
                if b < 2:
                    W("tensor", sdve, ph3_dve)
                    W("tensor", sact, ph3_act)
                else:
                    W("tensor", sact, b_evac[b - 2])
                for t4 in range(4):
                    E("tensor", lambda e, p=p, bi=bi, pb=pb, t4=t4: e.matmul(
                        PS[0:64, pb * 2048 + t4 * 512:
                           pb * 2048 + (t4 + 1) * 512],
                        h64_sb[:], b_in(p)[:, bi * 2048 + t4 * 512:
                                           bi * 2048 + (t4 + 1) * 512],
                        start=True, stop=True),
                        sem=(spe if t4 == 3 else None))
                b_pe[b] = spe.n
                W("scalar", spe, b_pe[b])
                if b >= 2:
                    W("scalar", sst, b_st[b - 2])
                elif b == 0:
                    W("scalar", sgst, sgst.n)
                E("scalar", lambda e, pb=pb: e.activation(
                    bw_out(pb), PS[0:64, pb * 2048:(pb + 1) * 2048],
                    AF.Copy), sem=sact)
                b_evac[b] = sact.n
                W("sync", sact, b_evac[b])
                E("sync", lambda e, b=b, pb=pb: e.dma_start(
                    w2p_st[:, b, :, :],
                    bw_out(pb).rearrange("a (dc dm) -> a dc dm", dm=128)),
                    sem=sst, k=16)
                b_st[b] = sst.n
            if g + 2 < 32:
                b_load(g + 2)
        return b_pe[127], sst.n, b_evac[127]

    slab_idx = [0]
    slab_ld = {}
    slab_consumed = {}

    def load_slab(key, src):
        i = slab_idx[0]
        assert i % 2 == 0
        slab_idx[0] += 2
        p = (i // 2) % 2
        for ii in (i, i + 1):
            if ii - NSLAB in slab_consumed:
                W("scalar", spe, slab_consumed[ii - NSLAB])
        E("scalar", lambda e, src=src, i=i: e.dma_start(
            wbuf[:, (i % NSLAB) * T:((i % NSLAB) + 2) * T], src),
            sem=sw[p], k=16)
        slab_ld[key] = (i, sw[p].n, p)
        return i

    xnd_v = xnd.rearrange("(j p) t -> p j t", p=128)
    ogs_v = ogs.rearrange("(k p) t -> p k t", p=128)

    pe_done = {}
    act_done = {}
    gp_done = {}
    dve_done = {}

    W("tensor", sgp, memset_done)
    W("tensor", sst, misc_done)
    W("vector", sst, misc_done)
    W("gpsimd", sgp, memset_done)

    g_dve = {}
    for s in range(NS):
        xrow = s * D

        mark(f"s{s}:Ph1a")
        if s == 0:
            conv_emit(4)
        xt_pair = xt.rearrange("(r p) t -> r p t", p=128)
        for jp in range(JD // 2):
            pp_ = jp % 2
            if s == 0:
                conv_emit(1)
            if jp >= 2:
                W("sync", sact, act_done[("1a", 2 * jp - 3)])
                W("sync", sgp, gp_done[("1a", 2 * jp - 3)])
            E("sync", lambda e, jp=jp, xrow=xrow: e.dma_start(
                pbuf_full(jp), xt_pair[(xrow // 128) + 2 * jp:
                                       (xrow // 128) + 2 * jp + 2].rearrange(
                    "r p t -> p r t")), sem=sx[pp_], k=16)
            for jj in range(2):
                j = 2 * jp + jj
                p = j % 2
                xin = pbuf(jp, jj)
                W("scalar", sx[pp_])
                if j >= 2:
                    W("scalar", spe, pe_done[("1a", j - 2)])
                E("scalar", lambda e, p=p, xin=xin: e.activation(
                    xbf_slot(p), xin[:], AF.Copy), sem=sact)
                act_done[("1a", j)] = sact.n
                W("gpsimd", sx[pp_])
                if j >= 2:
                    W("gpsimd", spe, pe_done[("1a", j - 2)])
                E("gpsimd", lambda e, p=p, xin=xin: e.tensor_tensor(
                    xsq_slot(p), xin[:], xin[:], ALU.mult), sem=sgp)
                gp_done[("1a", j)] = sgp.n
                W("tensor", sact, act_done[("1a", j)])
                W("tensor", sgp, gp_done[("1a", j)])
                for t4 in range(4):
                    E("tensor", lambda e, j=j, p=p, t4=t4: e.matmul(
                        PS[0:1, t4 * 512:(t4 + 1) * 512], ones_bf[:],
                        xbf_slot(p)[:, t4 * 512:(t4 + 1) * 512],
                        start=(j == 0), stop=(j == JD - 1)))
                for t4 in range(4):
                    E("tensor", lambda e, j=j, p=p, t4=t4: e.matmul(
                        PS[0:1, 2048 + t4 * 512:2048 + (t4 + 1) * 512],
                        ones_bf[:],
                        xsq_slot(p)[:, t4 * 512:(t4 + 1) * 512],
                        start=(j == 0), stop=(j == JD - 1)),
                        sem=(spe if t4 == 3 else None))
                pe_done[("1a", j)] = spe.n

        mark(f"s{s}:stats")
        W("vector", spe, pe_done[("1a", JD - 1)])
        E("vector", lambda e: e.tensor_scalar(
            m_row, PS[0:1, 0:2048], 1.0 / D, None, ALU.mult), sem=sdve)
        E("vector", lambda e: e.tensor_tensor(
            nminvs_row, m_row, m_row, ALU.mult), sem=sdve)
        E("vector", lambda e: e.scalar_tensor_tensor(
            PS[0:1, 0:2048], PS[0:1, 2048:4096], 1.0 / D, nminvs_row,
            ALU.mult, ALU.subtract), sem=sdve)
        dve_var = sdve.n
        W("scalar", sdve, dve_var)
        W("scalar", sgp, memset_done)
        E("scalar", lambda e: e.activation(
            invs_row, PS[0:1, 0:2048], AF.Sqrt, bias=eps_c[:]), sem=sact)
        act_sd = sact.n
        W("vector", sact, act_sd)

        def _recip(e):
            with nc.allow_low_precision(reason="bf16 normalizer row"):
                return e.reciprocal(invs_row, invs_row)
        E("vector", _recip, sem=sdve)
        E("vector", lambda e: e.scalar_tensor_tensor(
            nminvs_row, m_row, -1.0, invs_row, ALU.mult, ALU.mult), sem=sdve)
        dve_rows = sdve.n
        W("tensor", sdve, dve_rows)
        for t4 in range(4):
            E("tensor", lambda e, t4=t4: e.matmul(
                bank(t4), onesr_bf[:], invs_row[:, t4 * 512:(t4 + 1) * 512],
                start=True, stop=True))
        for t4 in range(4):
            E("tensor", lambda e, t4=t4: e.matmul(
                bank(4 + t4), onesr_bf[:],
                nminvs_row[:, t4 * 512:(t4 + 1) * 512],
                start=True, stop=True), sem=(spe if t4 == 3 else None))
        pe_rep = spe.n
        W("vector", spe, pe_rep)
        E("vector", lambda e: e.tensor_copy(sA[:], PS[:, 0:2048]), sem=sdve)
        E("vector", lambda e: e.tensor_copy(sII[:], PS[:, 2048:4096]),
          sem=sdve)
        dve_rep = sdve.n

        if s == 0:
            conv_gate("scalar", 1)
        load_slab(("wif", s, 0), wifb[0:128, :])
        load_slab(("wif", s, 1), wifb[128:256, :])

        mark(f"s{s}:Ph1b")
        xn_done = {}
        for jp in range(JD // 2):
            pp_ = jp % 2
            if s == 0:
                conv_emit(1)
            if jp <= 1:
                W("sync", sact, act_done[("1a", JD - 1)])
                W("sync", sgp, gp_done[("1a", JD - 1)])
            if jp >= 2:
                sm, cn = xn_done[2 * jp - 3]
                W("sync", sm, cn)
                sm, cn = xn_done[2 * jp - 4]
                W("sync", sm, cn)
            E("sync", lambda e, jp=jp, xrow=xrow: e.dma_start(
                pbuf_full(jp), xt_pair[(xrow // 128) + 2 * jp:
                                       (xrow // 128) + 2 * jp + 2].rearrange(
                    "r p t -> p r t")), sem=sx[pp_], k=16)
            for jj in range(2):
                j = 2 * jp + jj
                xin = pbuf(jp, jj)
                qn = "vector" if jj == 0 else "gpsimd"
                qsem = sdve if jj == 0 else sgp
                W(qn, sx[pp_])
                if j <= 1:
                    W(qn, sdve, dve_rep)
                E(qn, lambda e, xin=xin: e.tensor_tensor(
                    xin[:], xin[:], sA[:], ALU.mult))
                E(qn, lambda e, j=j, xin=xin: e.tensor_tensor(
                    xn_j(j), xin[:], sII[:], ALU.add), sem=qsem)
                xn_done[j] = (qsem, qsem.n)
                W("scalar", qsem, qsem.n)
                E("scalar", lambda e, j=j: e.dma_start(
                    xnd[j * 128:(j + 1) * 128, :], xn_j(j)), sem=sxn, k=16)
        xn_stores = sxn.n

        mark(f"s{s}:Ph2")
        W("gpsimd", sdve, dve_rep)
        E("gpsimd", lambda e: e.memset(sACC[:], 0.0), sem=sgp)
        acc_zero = sgp.n

        if s == 0:
            a_emit_load(0, xn_done)
            a_emit_load(1, xn_done)
        evac_act = {}
        evac_dve = {}
        scan_done = {}
        hs_store = {}
        for k in range(KE):
            iw, iwc, pw = slab_ld[("wif", s, k)]
            fw = iw + 1
            if s == 0:
                conv_emit(2)
            for qt in range(4):
                hof = qt * 512
                g3 = (k * 4 + qt) % 3
                b_i, b_f = 2 * g3, 2 * g3 + 1
                if k == 0 and qt == 0:
                    W("tensor", sdve, dve_rep)
                mprev = k * 4 + qt - 3
                if mprev >= 0:
                    kp, qp = divmod(mprev, 4)
                    W("tensor", sdve, evac_dve[(kp, qp)])
                    W("tensor", sact, evac_act[(kp, qp)])
                if qt == 0:
                    W("tensor", sw[pw], iwc)
                for j in range(JD):
                    if k == 0 and qt == 0:
                        W("tensor", xn_done[j][0], xn_done[j][1])
                    E("tensor", lambda e, j=j, b_i=b_i, iw=iw, hof=hof:
                      e.matmul(
                        bank(b_i), wslab(iw)[:, j * 128:(j + 1) * 128],
                        xn_j(j)[:, hof:hof + 512],
                        start=(j == 0), stop=(j == JD - 1)))
                    E("tensor", lambda e, j=j, b_f=b_f, fw=fw, hof=hof:
                      e.matmul(
                        bank(b_f), wslab(fw)[:, j * 128:(j + 1) * 128],
                        xn_j(j)[:, hof:hof + 512],
                        start=(j == 0), stop=(j == JD - 1)),
                        sem=(spe if j == JD - 1 else None))
                pe_done[("2", k, qt)] = spe.n
                if s == 0:
                    nq_ = (k * 4 + qt) * 2
                    a_emit_round(nq_, xn_done)
                    a_emit_round(nq_ + 1, xn_done)
                W("scalar", spe, pe_done[("2", k, qt)])
                if qt == 0 and k > 0:
                    W("scalar", sdve, scan_done[k - 1])
                E("scalar", lambda e, b_i=b_i, hof=hof: e.activation(
                    xinB[:, hof:hof + 512], bank(b_i), AF.Sigmoid))
                E("scalar", lambda e, b_f=b_f, hof=hof: e.activation(
                    sA[:, hof:hof + 512], bank(b_f), AF.Sigmoid), sem=sact)
                evac_act[(k, qt)] = sact.n
                W("vector", sact, evac_act[(k, qt)])
                E("vector", lambda e, hof=hof: e.tensor_scalar(
                    xinA[:, hof:hof + 512], sA[:, hof:hof + 512], -1.0, 1.0,
                    ALU.mult, ALU.add))
                E("vector", lambda e, b_i=b_i, hof=hof: e.tensor_tensor(
                    xinB[:, hof:hof + 512], bank(b_i),
                    xinB[:, hof:hof + 512], ALU.mult))
                E("vector", lambda e, hof=hof: e.tensor_tensor(
                    sII[:, hof:hof + 512], xinB[:, hof:hof + 512],
                    xinA[:, hof:hof + 512], ALU.mult), sem=sdve)
                evac_dve[(k, qt)] = sdve.n
            slab_consumed[iw] = pe_done[("2", k, 3)]
            slab_consumed[fw] = pe_done[("2", k, 3)]
            if k + 2 < KE:
                if s == 0:
                    conv_gate("scalar", 2 * ((k + 2) // 2) + 1)
                load_slab(("wif", s, k + 2),
                          wifb[(k + 2) * 128:(k + 3) * 128, :])
            if k >= 2:
                W("vector", sgp, gp_done[("2", k - 2)])
                W("vector", sst, hs_store[k - 2])
            E("vector", lambda e, k=k: e.tensor_tensor_scan(
                hsb(k), sA[:], sII[:], 0.0, ALU.mult, ALU.add), sem=sdve)
            scan_done[k] = sdve.n
            W("sync", sdve, scan_done[k])
            E("sync", lambda e, k=k: e.dma_start(
                hs[k * 128:(k + 1) * 128, :], hsb(k)), sem=sst, k=16)
            hs_store[k] = sst.n
            W("gpsimd", sdve, scan_done[k])
            if k == 0:
                W("gpsimd", sgp, acc_zero)
            E("gpsimd", lambda e, k=k: e.tensor_tensor(
                sH[:], hsb(k), hsb(k), ALU.mult))
            E("gpsimd", lambda e: e.tensor_tensor(
                sACC[:], sACC[:], sH[:], ALU.add), sem=sgp)
            gp_done[("2", k)] = sgp.n

        mark(f"s{s}:rms")
        W("tensor", sgp, gp_done[("2", KE - 1)])
        W("tensor", sdve, evac_dve[(KE - 1, 3)])
        W("tensor", sact, evac_act[(KE - 1, 3)])
        for t4 in range(4):
            E("tensor", lambda e, t4=t4: e.matmul(
                PS[0:1, t4 * 512:(t4 + 1) * 512], ones_f[:],
                sACC[:, t4 * 512:(t4 + 1) * 512],
                start=True, stop=True), sem=(spe if t4 == 3 else None))
        pe_ssq = spe.n
        W("scalar", spe, pe_ssq)
        E("scalar", lambda e: e.activation(
            invs_row, PS[0:1, 0:2048], AF.Sqrt, bias=eps_c[:],
            scale=1.0 / D), sem=sact)
        act_sd2 = sact.n
        W("vector", sact, act_sd2)

        def _recip2(e):
            with nc.allow_low_precision(reason="bf16 normalizer row"):
                return e.reciprocal(invs_row, invs_row)
        E("vector", _recip2, sem=sdve)
        dve_rstd = sdve.n
        W("tensor", sdve, dve_rstd)
        W("tensor", sdve, evac_dve[(KE - 1, 3)])
        W("tensor", sact, evac_act[(KE - 1, 3)])
        if s == 0:
            W("tensor", sact, aw_evac[127])
        for t4 in range(4):
            E("tensor", lambda e, t4=t4: e.matmul(
                bank(4 + t4), onesr_bf[:],
                invs_row[:, t4 * 512:(t4 + 1) * 512],
                start=True, stop=True), sem=(spe if t4 == 3 else None))
        pe_rstd = spe.n
        W("vector", spe, pe_rstd)
        W("vector", sgp, gp_done[("2", KE - 1)])
        E("vector", lambda e: e.tensor_copy(sACC[:], PS[:, 2048:4096]),
          sem=sdve)
        dve_rstd_rep = sdve.n

        mark(f"s{s}:Ph3")
        if s == 0:
            conv_gate("scalar", CONV_WG0 + 0)
        load_slab(("wg", s, 0),
                  wgb[0:256, :].rearrange("(n p) d -> p n d", p=128))
        og_store = {}
        hsin_ld = {}
        for k in range(2):
            W("sync", sst, hs_store[KE - 1])
            W("sync", sgp, gp_done[("2", KE - 1)])
            E("sync", lambda e, k=k: e.dma_start(
                hsb(k), hs[k * 128:(k + 1) * 128, :]), sem=sq[0], k=16)
            hsin_ld[k] = sq[0].n
        for k in range(KE):
            if s == 0:
                conv_emit(4)
            gw0, gwc, pg = slab_ld[("wg", s, k - k % 2)]
            gw = gw0 + k % 2
            base = 0 if k % 2 == 0 else 4
            if k % 2 == 0:
                W("tensor", sw[pg], gwc)
            if k == 0:
                W("tensor", sact, act_sd2)
            if k == 1:
                W("tensor", sdve, dve_rstd_rep)
            if k >= 2:
                W("tensor", sdve, g_dve[(s, k - 2)])
            for j in range(JD):
                for t4 in range(4):
                    E("tensor", lambda e, j=j, base=base, gw=gw, t4=t4:
                      e.matmul(
                        bank(base + t4),
                        wslab(gw)[:, j * 128:(j + 1) * 128],
                        xn_j(j)[:, t4 * 512:(t4 + 1) * 512],
                        start=(j == 0), stop=(j == JD - 1)),
                        sem=(spe if (j == JD - 1 and t4 == 3) else None))
            pe_done[("3", k)] = spe.n
            slab_consumed[gw] = pe_done[("3", k)]
            if k % 2 == 0 and k + 2 < KE:
                if s == 0:
                    conv_gate("scalar", CONV_WG0 + (k + 2) // 2)
                load_slab(("wg", s, k + 2),
                          wgb[(k + 2) * 128:(k + 4) * 128, :].rearrange(
                              "(n p) d -> p n d", p=128))
            W("scalar", spe, pe_done[("3", k)])
            E("scalar", lambda e, base=base: e.activation(
                xinB[:], PS[:, base * 512:(base + 4) * 512], AF.Sigmoid),
                sem=sact)
            act_done[("3", k)] = sact.n
            W("vector", sact, act_done[("3", k)])
            E("vector", lambda e, base=base: e.tensor_tensor(
                xinA[:], PS[:, base * 512:(base + 4) * 512], xinB[:],
                ALU.mult))
            E("vector", lambda e, k=k: e.scalar_tensor_tensor(
                xinB[:], xinA[:], gnw_sb[:, k:k + 1], sACC[:], ALU.mult,
                ALU.mult))
            W("vector", sq[0], hsin_ld[k])
            if k >= 2:
                W("vector", sst, og_store[k - 2])
            E("vector", lambda e, k=k: e.tensor_tensor(
                ogo(k), xinB[:], hsb(k), ALU.mult), sem=sdve)
            g_dve[(s, k)] = sdve.n
            W("sync", sdve, g_dve[(s, k)])
            E("sync", lambda e, k=k: e.dma_start(
                ogs[k * 128:(k + 1) * 128, :], ogo(k)), sem=sst, k=16)
            og_store[k] = sst.n
            if k + 2 < KE:
                E("sync", lambda e, k=k: e.dma_start(
                    hsb(k + 2), hs[(k + 2) * 128:(k + 3) * 128, :]),
                    sem=sq[0], k=16)
                hsin_ld[k + 2] = sq[0].n
        og_all = sst.n
        if s == 0:
            conv_emit(96)
            b_pe_last, b_st_last, b_evac_last = b_emit_block(
                g_dve[(s, KE - 1)], act_done[("3", KE - 1)])

        ring_pe = {}
        ring_i = [0]
        q_xnq_done = {}
        hid_dve = {}
        out_dve = {}
        outst = {}
        pe_4b_last = {}
        xnq_lds = {}
        for q in range(NQ):
            pq = q % 2
            if q == 0:
                W("sync", sst, og_all)
                W("sync", sxn, xn_stores)
                E("sync", lambda e: e.dma_start(
                    xnq_slot(), xnd_v[:, :, 0:TQ]), sem=sq[0], k=16)
                xnq_lds[0] = sq[0].n
            xnq_ld = xnq_lds[q]
            if q > 0:
                W("sync", spe, pe_4b_last[q - 1])
            if s == 0 and q == 0:
                W("sync", spe, b_pe_last)
            W("sync", sst, og_all)
            E("sync", lambda e, q=q: e.dma_start(
                ogq_slot(), ogs_v[:, :, q * TQ:(q + 1) * TQ]),
                sem=sq[1], k=16)
            ogq_ld = sq[1].n

            mark(f"s{s}:Ph4a{q}")
            if s == 0 and q == 0:
                conv_gate("scalar", CONV_W13 + 1)
            if s == 0 and q == 1:
                W("scalar", sgst, sgst.n)
            load_slab(("w13", s, q, 0), w13b[0:128, :])
            load_slab(("w13", s, q, 1), w13b[128:256, :])
            for hc in range(KH):
                ph = hc % 2
                b_ug, b_ul = (0, 1) if ph == 0 else (2, 3)
                i1, c1, p1 = slab_ld[("w13", s, q, hc)]
                i3 = i1 + 1
                W("tensor", sw[p1], c1)
                if hc == 0:
                    W("tensor", sq[0], xnq_ld)
                    if q == 0:
                        W("tensor", sdve, g_dve[(s, KE - 1)])
                        if s == 0:
                            W("tensor", sact, b_evac_last)
                if (q, hc - 2) in hid_dve:
                    W("tensor", sdve, hid_dve[(q, hc - 2)])
                for j in range(JD):
                    E("tensor", lambda e, j=j, b_ug=b_ug, i1=i1: e.matmul(
                        bank(b_ug), wslab(i1)[:, j * 128:(j + 1) * 128],
                        xnq_slot()[:, j * TQ:(j + 1) * TQ],
                        start=(j == 0), stop=(j == JD - 1)))
                    E("tensor", lambda e, j=j, b_ul=b_ul, i3=i3: e.matmul(
                        bank(b_ul), wslab(i3)[:, j * 128:(j + 1) * 128],
                        xnq_slot()[:, j * TQ:(j + 1) * TQ],
                        start=(j == 0), stop=(j == JD - 1)),
                        sem=(spe if j == JD - 1 else None))
                pe_done[("4a", q, hc)] = spe.n
                slab_consumed[i1] = spe.n
                slab_consumed[i3] = spe.n
                if hc + 2 < KH:
                    if s == 0 and q == 0:
                        conv_gate("scalar",
                                  CONV_W13 + 2 * ((hc + 2) // 2) + 1)
                    load_slab(("w13", s, q, hc + 2),
                              w13b[(hc + 2) * 128:(hc + 3) * 128, :])
                W("scalar", spe, pe_done[("4a", q, hc)])
                E("scalar", lambda e, b_ug=b_ug, ph=ph: e.activation(
                    sg_slot(ph), bank(b_ug), AF.Sigmoid), sem=sact)
                act_done[("4a", q, hc)] = sact.n
                W("vector", sact, act_done[("4a", q, hc)])
                E("vector", lambda e, b_ug=b_ug, ph=ph: e.tensor_tensor(
                    tmp_slot(ph), bank(b_ug), sg_slot(ph), ALU.mult))
                E("vector", lambda e, b_ul=b_ul, ph=ph, hc=hc:
                  e.tensor_tensor(
                    hidq_h(hc), tmp_slot(ph), bank(b_ul), ALU.mult),
                    sem=sdve)
                hid_dve[(q, hc)] = sdve.n
            q_xnq_done[q] = pe_done[("4a", q, KH - 1)]

            mark(f"s{s}:Ph4b{q}")
            for dc in range(JD):
                pd = dc % 2
                b_out = 6 + pd
                key = (q, dc - 2) if dc >= 2 else (q - 1, JD - 2 + dc)
                if key in outst:
                    W("gpsimd", sout, outst[key])
                if q == 0 and dc < 2:
                    W("gpsimd", sdve, g_dve[(s, KE - 1)])
                E("gpsimd", lambda e, dc=dc, q=q, pd=pd, xrow=xrow:
                  e.dma_start(
                    outb_slot(pd),
                    xt[xrow + dc * 128:xrow + (dc + 1) * 128,
                       q * TQ:(q + 1) * TQ]), sem=sx[pd], k=16)
                xres_ld = sx[pd].n
                if dc == 0:
                    W("tensor", sdve, hid_dve[(q, KH - 1)])
                    W("tensor", sq[1], ogq_ld)
                    if q == 0:
                        W("sync", sst, og_all)
                    if q + 1 < NQ:
                        W("sync", spe, q_xnq_done[q])
                        W("sync", sxn, xn_stores)
                        E("sync", lambda e, q=q: e.dma_start(
                            xnq_slot(),
                            xnd_v[:, :, (q + 1) * TQ:(q + 2) * TQ]),
                            sem=sq[0], k=16)
                        xnq_lds[q + 1] = sq[0].n
                if (q, dc - 2) in out_dve:
                    W("tensor", sdve, out_dve[(q, dc - 2)])
                elif (q - 1, JD - 2 + dc) in out_dve:
                    W("tensor", sdve, out_dve[(q - 1, JD - 2 + dc)])
                for gi in range(3):
                    ri = ring_i[0]
                    ring_i[0] += 1
                    if ri - 4 in ring_pe:
                        W("sync", spe, ring_pe[ri - 4])
                    if gi < 2:
                        if s == 0 and q == 0 and dc == 0 and gi == 0:
                            W("sync", sst, b_st_last)
                            W("sync", sgst, sgst.n)
                        E("sync", lambda e, dc=dc, gi=gi, ri=ri: e.dma_start(
                            ring_slot(ri),
                            w2p[dc * 128:(dc + 1) * 128,
                                gi * 4096:(gi + 1) * 4096]),
                            sem=sr[ri % 2], k=16)
                    else:
                        if s == 0 and q == 0:
                            conv_gate("sync", CONV_WO + dc // 2)
                        E("sync", lambda e, dc=dc, ri=ri: e.dma_start(
                            ring_slot(ri)[:, 0:2048],
                            wob[dc * 128:(dc + 1) * 128, :]),
                            sem=sr[ri % 2], k=16)
                    rc = sr[ri % 2].n
                    W("tensor", sr[ri % 2], rc)
                    nt = 32 if gi < 2 else 16
                    for ti in range(nt):
                        if gi < 2:
                            hc = gi * 32 + ti
                            E("tensor", lambda e, b_out=b_out, ri=ri, ti=ti,
                              hc=hc: e.matmul(
                                bank(b_out),
                                ring_slot(ri)[:, ti * 128:(ti + 1) * 128],
                                hidq_h(hc),
                                start=(hc == 0), stop=False),
                                sem=(spe if ti == nt - 1 else None))
                        else:
                            E("tensor", lambda e, b_out=b_out, ri=ri, ti=ti:
                              e.matmul(
                                bank(b_out),
                                ring_slot(ri)[:, ti * 128:(ti + 1) * 128],
                                ogq_slot()[:, ti * TQ:(ti + 1) * TQ],
                                start=False, stop=(ti == 15)),
                                sem=(spe if ti == 15 else None))
                    ring_pe[ri] = spe.n
                pe_out = spe.n
                pe_4b_last[q] = pe_out
                W("vector", spe, pe_out)
                W("vector", sx[pd], xres_ld)
                E("vector", lambda e, b_out=b_out, pd=pd: e.tensor_tensor(
                    outb_slot(pd), bank(b_out), outb_slot(pd), ALU.add),
                    sem=sdve)
                out_dve[(q, dc)] = sdve.n
                W("gpsimd", sdve, out_dve[(q, dc)])
                E("gpsimd", lambda e, dc=dc, q=q, pd=pd, xrow=xrow:
                  e.dma_start(
                    out[xrow + dc * 128:xrow + (dc + 1) * 128,
                        q * TQ:(q + 1) * TQ], outb_slot(pd)),
                    sem=sout, k=16)
                outst[(q, dc)] = sout.n

        for qn in PROG:
            W(qn, spe, spe.n)
            W(qn, sact, sact.n)
            W(qn, sdve, sdve.n)
            W(qn, sgp, sgp.n)
            W(qn, sout, sout.n)

    W("sync", sout, sout.n)

    with nc.Block() as block:
        @block.sync
        def _(eng):
            for f in PROG["sync"]:
                f(eng)

        @block.gpsimd
        def _(eng):
            for f in PROG["gpsimd"]:
                f(eng)

        @block.scalar
        def _(eng):
            for f in PROG["scalar"]:
                f(eng)

        @block.vector
        def _(eng):
            for f in PROG["vector"]:
                f(eng)

        @block.tensor
        def _(eng):
            for f in PROG["tensor"]:
                f(eng)

    return nc


def kernel(L0_attn_hidden_states, L0_attn_Wi, L0_attn_Wf, L0_attn_Wg,
           L0_attn_Wo, L0_attn_g_norm_weight, L0_ffn_W1, L0_ffn_W2,
           L0_ffn_W3):
    x = np.asarray(L0_attn_hidden_states, dtype=np.float32)
    Wi = np.asarray(L0_attn_Wi, dtype=np.float32)
    Wf = np.asarray(L0_attn_Wf, dtype=np.float32)
    Wg = np.asarray(L0_attn_Wg, dtype=np.float32)
    Wo = np.asarray(L0_attn_Wo, dtype=np.float32)
    gnwv = np.asarray(L0_attn_g_norm_weight, dtype=np.float32)
    W1 = np.asarray(L0_ffn_W1, dtype=np.float32)
    W2 = np.asarray(L0_ffn_W2, dtype=np.float32)
    W3 = np.asarray(L0_ffn_W3, dtype=np.float32)

    xt_all = np.ascontiguousarray(x.transpose(0, 2, 1))

    def tile_de(W):
        return np.ascontiguousarray(
            W.reshape(16, 128, 16, 128).transpose(2, 1, 0, 3).reshape(
                2048, 2048))

    def tile_hd(W):
        return np.ascontiguousarray(
            W.reshape(64, 128, 16, 128).transpose(0, 3, 2, 1).reshape(
                8192, 2048))

    wir = tile_de(Wi)
    wfr = tile_de(Wf)
    wgr = tile_de(Wg)
    wor = tile_de(Wo)
    w1r = tile_hd(W1)
    w3r = tile_hd(W3)
    w2tt = np.ascontiguousarray(W2.T)
    h128v = np.ascontiguousarray(
        (_hadamard(128) * (2.0 ** -7)).astype(ml_dtypes.bfloat16))
    h64v = np.ascontiguousarray(
        (_hadamard(64) * np.sqrt(2.0)).astype(ml_dtypes.bfloat16))
    gnw2 = np.ascontiguousarray(gnwv.reshape(2048, 1))

    nc = build_nc()
    in_maps = []
    for c in range(8):
        in_maps.append({
            "xt": np.ascontiguousarray(
                xt_all[2 * c:2 * c + 2].reshape(NS * D, T)),
            "wir": wir, "wfr": wfr, "wgr": wgr, "wor": wor,
            "w1r": w1r, "w3r": w3r, "w2t": w2tt,
            "gnw": gnw2, "h128": h128v, "h64": h64v,
        })
    res = run_bass_kernel_spmd(nc, in_maps, core_ids=list(range(8)))
    outs = [res.results[c]["out"].reshape(NS, D, T) for c in range(8)]
    full = np.concatenate(outs, axis=0)
    return np.ascontiguousarray(full.transpose(0, 2, 1))


# version 26
# speedup vs baseline: 1.1438x; 1.0524x over previous
import numpy as np
import ml_dtypes
import concourse.bass as bass
import concourse.mybir as mybir
from concourse.bass_utils import run_bass_kernel_spmd

F32 = mybir.dt.float32
BF16 = mybir.dt.bfloat16
AF = mybir.ActivationFunctionType
ALU = mybir.AluOpType

D = 2048
T = 2048
H = 8192
JD = 16
KE = 16
KH = 64
TQ = 512
NQ = T // TQ
NS = 2
EPS = 1e-5
TMARKS = []


def _hadamard(n):
    i = np.arange(n)
    pc = np.vectorize(lambda v: bin(v).count("1"))(i[:, None] & i[None, :])
    return np.where(pc % 2 == 0, 1.0, -1.0).astype(np.float32)


class Sem:
    def __init__(self, nc, name):
        self.h = nc.alloc_semaphore(name)
        self.n = 0


def build_nc():
    nc = bass.Bass("TRN2", target_bir_lowering=False)

    xt = nc.dram_tensor("xt", [NS * D, T], F32, kind="ExternalInput").ap()
    wir = nc.dram_tensor("wir", [D, D], F32, kind="ExternalInput").ap()
    wfr = nc.dram_tensor("wfr", [D, D], F32, kind="ExternalInput").ap()
    wgr = nc.dram_tensor("wgr", [D, D], F32, kind="ExternalInput").ap()
    wor = nc.dram_tensor("wor", [D, D], F32, kind="ExternalInput").ap()
    w1r = nc.dram_tensor("w1r", [H, D], F32, kind="ExternalInput").ap()
    w3r = nc.dram_tensor("w3r", [H, D], F32, kind="ExternalInput").ap()
    w2t = nc.dram_tensor("w2t", [H, D], F32, kind="ExternalInput").ap()
    gnw = nc.dram_tensor("gnw", [D, 1], F32, kind="ExternalInput").ap()
    h128 = nc.dram_tensor("h128", [128, 128], BF16, kind="ExternalInput").ap()
    h64 = nc.dram_tensor("h64", [64, 64], BF16, kind="ExternalInput").ap()
    out = nc.dram_tensor("out", [NS * D, T], F32, kind="ExternalOutput").ap()
    w13b = nc.dram_tensor("w13b", [H, 2 * D], BF16, kind="Internal").ap()
    wifb = nc.dram_tensor("wifb", [D, 2 * D], BF16, kind="Internal").ap()
    wgb = nc.dram_tensor("wgb", [D, D], BF16, kind="Internal").ap()
    w2i = nc.dram_tensor("w2i", [H, D], BF16, kind="Internal").ap()
    w2p = nc.dram_tensor("w2p", [D, H], BF16, kind="Internal").ap()
    w2q = nc.dram_tensor("w2q", [H, D], BF16, kind="Internal").ap()
    wob = nc.dram_tensor("wob", [D, D], BF16, kind="Internal").ap()
    xnd = nc.dram_tensor("xnd", [D, T], BF16, kind="Internal").ap()
    ogs = nc.dram_tensor("ogs", [D, T], BF16, kind="Internal").ap()
    hs = nc.dram_tensor("hs", [D, T], BF16, kind="Internal").ap()

    R1 = nc.alloc_sbuf_tensor("R1", [128, JD * T], BF16)
    R2 = nc.alloc_sbuf_tensor("R2", [128, KE * T], BF16)
    sA = nc.alloc_sbuf_tensor("sA", [128, T], F32)
    sII = nc.alloc_sbuf_tensor("sII", [128, T], F32)
    sH = nc.alloc_sbuf_tensor("sH", [128, T], F32)
    sACC = nc.alloc_sbuf_tensor("sACC", [128, T], F32)
    xin2 = nc.alloc_sbuf_tensor("xin2", [128, 2 * T], F32)
    xinA = xin2[:, 0:T]
    xinB = xin2[:, T:2 * T]
    wbuf = nc.alloc_sbuf_tensor("wbuf", [128, 4 * T], BF16)
    rows = nc.alloc_sbuf_tensor("rows", [1, 3 * T], BF16)
    gnw_sb = nc.alloc_sbuf_tensor("gnw_sb", [128, KE], F32)
    h64_sb = nc.alloc_sbuf_tensor("h64_sb", [64, 64], BF16)
    h128_sb = nc.alloc_sbuf_tensor("h128_sb", [128, 128], BF16)
    ones_bf = nc.alloc_sbuf_tensor("ones_bf", [128, 1], BF16)
    ones_f = nc.alloc_sbuf_tensor("ones_f", [128, 1], F32)
    onesr_bf = nc.alloc_sbuf_tensor("onesr_bf", [1, 128], BF16)
    eps_c = nc.alloc_sbuf_tensor("eps_c", [1, 1], F32)

    PS = nc.alloc_psum_tensor("PS", [128, 4096], F32)

    def bank(i, w=512):
        return PS[:, i * 512:i * 512 + w]

    m_row = rows[:, 0:T]
    invs_row = rows[:, T:2 * T]
    nminvs_row = rows[:, 2 * T:3 * T]

    def xn_j(j):
        return R1[:, j * T:(j + 1) * T]

    def hidq_h(hc):
        return R1[:, hc * TQ:(hc + 1) * TQ]

    def xbf_slot(p):
        return R2[:, p * T:(p + 1) * T]

    def xsq_slot(p):
        return R2[:, (2 + p) * T:(3 + p) * T]

    def conv_slot(p):
        return R2[:, (8 + 4 * p) * T:(12 + 4 * p) * T]

    def hsb(k):
        return R2[:, (k % 2) * T:((k % 2) + 1) * T]

    def ogo(k):
        return R2[:, (2 + k % 2) * T:(3 + k % 2) * T]

    def xnq_slot():
        return R2[:, 0:JD * TQ]

    def ogq_slot():
        return R2[:, JD * TQ:2 * JD * TQ]

    def ring_slot(i):
        off = 2 * JD * TQ
        return R2[:, off + (i % 4) * 2 * T:off + ((i % 4) + 1) * 2 * T]

    NSLAB = 4

    def wslab(i):
        return wbuf[:, (i % NSLAB) * T:((i % NSLAB) + 1) * T]

    R2f = R2.bitcast(F32)

    def pbuf(i, jj):
        base = 4096 + (i % 2) * 4096
        return R2f[:, base + jj * T:base + (jj + 1) * T]

    def pbuf_full(i):
        base = 4096 + (i % 2) * 4096
        return R2f[:, base:base + 2 * T]

    def sg_slot(ph):
        return xinA[:, ph * TQ:(ph + 1) * TQ]

    def tmp_slot(ph):
        return xinB[:, ph * TQ:(ph + 1) * TQ]

    def outb_slot(pd):
        return xinA[:, (2 + pd) * TQ:(3 + pd) * TQ]

    sx = [Sem(nc, "sx0"), Sem(nc, "sx1")]
    sw = [Sem(nc, "sw0"), Sem(nc, "sw1")]
    sr = [Sem(nc, "sr0"), Sem(nc, "sr1")]
    sq = [Sem(nc, "sq0"), Sem(nc, "sq1")]
    sst = Sem(nc, "sst")
    sxn = Sem(nc, "sxn")
    srl = Sem(nc, "srl")
    sout = Sem(nc, "sout")
    spe = Sem(nc, "spe")
    sact = Sem(nc, "sact")
    sdve = Sem(nc, "sdve")
    sgp = Sem(nc, "sgp")

    PROG = {q: [] for q in ("sync", "gpsimd", "scalar", "vector", "tensor")}
    nmm = [0]

    def mark(label):
        TMARKS.append((label, nmm[0]))

    def E(q, fn, sem=None, k=1):
        if sem is not None:
            sem.n += k
        if q == "tensor":
            nmm[0] += 1

        def f(e, fn=fn, sem=sem, k=k):
            inst = fn(e)
            if sem is not None:
                inst.then_inc(sem.h, k)
        PROG[q].append(f)

    def W(q, sem, val=None):
        v = sem.n if val is None else val
        if v is None or v <= 0:
            return
        PROG[q].append(lambda e, s=sem.h, v=v: e.wait_ge(s, v))

    E("gpsimd", lambda e: e.memset(ones_bf[:], 1.0))
    E("gpsimd", lambda e: e.memset(ones_f[:], 1.0))
    E("gpsimd", lambda e: e.memset(onesr_bf[:], 1.0))
    E("gpsimd", lambda e: e.memset(eps_c[:], EPS), sem=sgp)
    memset_done = sgp.n
    E("sync", lambda e: e.dma_start(h128_sb[:], h128), sem=sst, k=16)
    E("sync", lambda e: e.dma_start(h64_sb[:], h64), sem=sst, k=16)
    for k in range(KE):
        E("sync", lambda e, k=k: e.dma_start(
            gnw_sb[:, k:k + 1], gnw[k * 128:(k + 1) * 128, :]), sem=sst, k=16)
    misc_done = sst.n

    scv = [Sem(nc, "scv0"), Sem(nc, "scv1")]
    sgst = Sem(nc, "sgst")
    saw = [Sem(nc, "saw0"), Sem(nc, "saw1")]

    conv_jobs = []
    for i in range(8):
        conv_jobs.append((wir[i * 256:(i + 1) * 256, :],
                          wifb[i * 256:(i + 1) * 256, 0:D]))
        conv_jobs.append((wfr[i * 256:(i + 1) * 256, :],
                          wifb[i * 256:(i + 1) * 256, D:2 * D]))
    CONV_WG0 = len(conv_jobs)
    for i in range(8):
        conv_jobs.append((wgr[i * 256:(i + 1) * 256, :],
                          wgb[i * 256:(i + 1) * 256, :]))
    CONV_W13 = len(conv_jobs)
    for i in range(32):
        conv_jobs.append((w1r[i * 256:(i + 1) * 256, :],
                          w13b[i * 256:(i + 1) * 256, 0:D]))
        conv_jobs.append((w3r[i * 256:(i + 1) * 256, :],
                          w13b[i * 256:(i + 1) * 256, D:2 * D]))
    CONV_WO = len(conv_jobs)
    for i in range(8):
        conv_jobs.append((wor[i * 256:(i + 1) * 256, :],
                          wob[i * 256:(i + 1) * 256, :]))
    conv_idx = [0]
    conv_store = {}

    def conv_slot2(i):
        return R2[:, (12 + 2 * (i % 2)) * T:(14 + 2 * (i % 2)) * T]

    def conv_emit(n):
        for _ in range(n):
            i = conv_idx[0]
            if i >= len(conv_jobs):
                return
            conv_idx[0] += 1
            src_, dst_ = conv_jobs[i]
            sv = src_.rearrange("(n p) d -> p n d", p=128)
            dv = dst_.rearrange("(n p) d -> p n d", p=128)
            p = i % 2
            if i >= 2:
                W("gpsimd", sgst, conv_store[i - 2])
            E("gpsimd", lambda e, sv=sv, p=p, i=i: e.dma_start(
                conv_slot2(p), sv), sem=scv[p], k=16)
            W("gpsimd", scv[p])
            E("gpsimd", lambda e, dv=dv, p=p, i=i: e.dma_start(
                dv, conv_slot2(p)), sem=sgst, k=16)
            conv_store[i] = sgst.n

    def conv_gate(q, i):
        if i in conv_store:
            W(q, sgst, conv_store[i])
        else:
            raise RuntimeError(f"conv job {i} not yet emitted")

    aw_ld = {}
    aw_pe = {}
    aw_evac = {}
    aw_st = {}

    def a_in(p):
        return R2[:, (4 + 2 * p) * T:(6 + 2 * p) * T]

    def a_out(n):
        return R2[:, 8 * T + (n % 4) * 1024:8 * T + (n % 4) * 1024 + 1024]

    def a_emit_load(g, xn_done):
        p = g % 2
        if g < 2:
            sm, cn = xn_done[13]
            W("gpsimd", sm, cn)
            sm, cn = xn_done[12]
            W("gpsimd", sm, cn)
        else:
            W("gpsimd", spe, aw_pe[(g - 2) * 4 + 3])
        E("gpsimd", lambda e, g=g, p=p: e.dma_start(
            a_in(p), w2t[g * 256:(g + 1) * 256, :].rearrange(
                "(n p) d -> p n d", p=128)), sem=saw[p], k=16)
        aw_ld[g] = saw[p].n

    def a_emit_round(n, xn_done):
        a = n // 2
        r = n % 2
        g = a // 2
        p = g % 2
        col = (a % 2) * 2048 + r * 1024
        if n % 4 == 0:
            W("tensor", saw[p], aw_ld[g])
        if n >= 1:
            W("tensor", sact, aw_evac[n - 1])
        for t2 in range(2):
            E("tensor", lambda e, p=p, col=col, t2=t2: e.matmul(
                bank(6 + t2), h128_sb[:],
                a_in(p)[:, col + t2 * 512:col + (t2 + 1) * 512],
                start=True, stop=True),
                sem=(spe if t2 == 1 else None))
        aw_pe[n] = spe.n
        W("scalar", spe, aw_pe[n])
        if n >= 4:
            W("scalar", sst, aw_st[n - 4])
        elif n == 0:
            sm, cn = xn_done[15]
            W("scalar", sm, cn)
            sm, cn = xn_done[14]
            W("scalar", sm, cn)
        E("scalar", lambda e, n=n: e.activation(
            a_out(n), PS[:, 6 * 512:8 * 512], AF.Copy), sem=sact)
        aw_evac[n] = sact.n
        W("sync", sact, aw_evac[n])
        E("sync", lambda e, a=a, r=r, n=n: e.dma_start(
            w2i[a * 128:(a + 1) * 128, r * 1024:(r + 1) * 1024], a_out(n)),
            sem=sst, k=16)
        aw_st[n] = sst.n
        if n % 4 == 3 and g + 2 < 32:
            a_emit_load(g + 2, xn_done)

    w2i_ab = w2i.rearrange("(a b) d -> a b d", b=128)
    w2p_st = w2p.rearrange("(dc p) (a dm) -> a p dc dm", p=128, dm=128)

    def b_in4(p):
        return R2[0:64, (4 + 4 * p) * T:(8 + 4 * p) * T]

    def bw_out(p):
        return R2[0:64, p * T:(p + 1) * T]

    def b_emit_block(ph3_dve, ph3_act):
        b_ld = {}
        b_pe = {}
        b_evac = {}
        b_st = {}

        def b_load(g):
            p = g % 2
            if g < 2:
                W("sync", spe, aw_pe[127])
                W("sync", sst, aw_st[127])
                W("sync", sdve, ph3_dve)
            else:
                W("sync", spe, b_pe[4 * (g - 2) + 3])
            E("sync", lambda e, g=g, p=p: e.dma_start(
                b_in4(p), w2i_ab[:, 4 * g:4 * g + 4, :]), sem=sr[p], k=16)
            b_ld[g] = sr[p].n

        b_load(0)
        b_load(1)
        for g in range(32):
            p = g % 2
            for bi in range(4):
                b = 4 * g + bi
                pb = b % 2
                if bi == 0:
                    W("tensor", sr[p], b_ld[g])
                if b < 2:
                    W("tensor", sdve, ph3_dve)
                    W("tensor", sact, ph3_act)
                else:
                    W("tensor", b_evac[b - 2][0], b_evac[b - 2][1])
                for t4 in range(4):
                    E("tensor", lambda e, p=p, bi=bi, pb=pb, t4=t4: e.matmul(
                        PS[0:64, pb * 2048 + t4 * 512:
                           pb * 2048 + (t4 + 1) * 512],
                        h64_sb[:], b_in4(p)[:, bi * 2048 + t4 * 512:
                                            bi * 2048 + (t4 + 1) * 512],
                        start=True, stop=True),
                        sem=(spe if t4 == 3 else None))
                b_pe[b] = spe.n
                eq = "scalar" if pb == 0 else "vector"
                esem = sact if pb == 0 else sdve
                W(eq, spe, b_pe[b])
                if b >= 2:
                    W(eq, sst, b_st[b - 2])
                else:
                    W(eq, sdve, ph3_dve)
                if pb == 0:
                    E(eq, lambda e, pb=pb: e.activation(
                        bw_out(pb), PS[0:64, pb * 2048:(pb + 1) * 2048],
                        AF.Copy), sem=esem)
                else:
                    E(eq, lambda e, pb=pb: e.tensor_copy(
                        bw_out(pb), PS[0:64, pb * 2048:(pb + 1) * 2048]),
                        sem=esem)
                b_evac[b] = (esem, esem.n)
                W("sync", esem, esem.n)
                E("sync", lambda e, b=b, pb=pb: e.dma_start(
                    w2q[b * 64:(b + 1) * 64, :], bw_out(pb)), sem=sst, k=16)
                b_st[b] = sst.n
            if g + 2 < 32:
                b_load(g + 2)
        w2q_v = w2q.rearrange("(b a) d -> b a d", a=64)
        rl = {}
        for dc in range(JD):
            W("sync", sst, b_st[127])
            E("sync", lambda e, dc=dc: e.dma_start(
                w2p[dc * 128:(dc + 1) * 128, :],
                w2q_v[:, :, dc * 128:(dc + 1) * 128]), sem=srl, k=16)
            rl[dc] = srl.n
        return b_pe[127], rl, b_evac[127], b_st[127]

    slab_idx = [0]
    slab_ld = {}
    slab_consumed = {}

    def load_slab(key, src):
        i = slab_idx[0]
        assert i % 2 == 0
        slab_idx[0] += 2
        p = (i // 2) % 2
        for ii in (i, i + 1):
            if ii - NSLAB in slab_consumed:
                W("scalar", spe, slab_consumed[ii - NSLAB])
        E("scalar", lambda e, src=src, i=i: e.dma_start(
            wbuf[:, (i % NSLAB) * T:((i % NSLAB) + 2) * T], src),
            sem=sw[p], k=16)
        slab_ld[key] = (i, sw[p].n, p)
        return i

    xnd_v = xnd.rearrange("(j p) t -> p j t", p=128)
    ogs_v = ogs.rearrange("(k p) t -> p k t", p=128)

    pe_done = {}
    act_done = {}
    gp_done = {}
    dve_done = {}

    W("tensor", sgp, memset_done)
    W("tensor", sst, misc_done)
    W("vector", sst, misc_done)
    W("gpsimd", sgp, memset_done)

    g_dve = {}
    for s in range(NS):
        xrow = s * D

        mark(f"s{s}:Ph1a")
        if s == 0:
            conv_emit(2)
        xt_pair = xt.rearrange("(r p) t -> r p t", p=128)
        for jp in range(JD // 2):
            pp_ = jp % 2
            if s == 0:
                conv_emit(1)
            if jp >= 2:
                W("sync", sact, act_done[("1a", 2 * jp - 3)])
                W("sync", sgp, gp_done[("1a", 2 * jp - 3)])
            E("sync", lambda e, jp=jp, xrow=xrow: e.dma_start(
                pbuf_full(jp), xt_pair[(xrow // 128) + 2 * jp:
                                       (xrow // 128) + 2 * jp + 2].rearrange(
                    "r p t -> p r t")), sem=sx[pp_], k=16)
            for jj in range(2):
                j = 2 * jp + jj
                p = j % 2
                xin = pbuf(jp, jj)
                W("scalar", sx[pp_])
                if j >= 2:
                    W("scalar", spe, pe_done[("1a", j - 2)])
                E("scalar", lambda e, p=p, xin=xin: e.activation(
                    xbf_slot(p), xin[:], AF.Copy), sem=sact)
                act_done[("1a", j)] = sact.n
                W("gpsimd", sx[pp_])
                if j >= 2:
                    W("gpsimd", spe, pe_done[("1a", j - 2)])
                E("gpsimd", lambda e, p=p, xin=xin: e.tensor_tensor(
                    xsq_slot(p), xin[:], xin[:], ALU.mult), sem=sgp)
                gp_done[("1a", j)] = sgp.n
                W("tensor", sact, act_done[("1a", j)])
                W("tensor", sgp, gp_done[("1a", j)])
                for t4 in range(4):
                    E("tensor", lambda e, j=j, p=p, t4=t4: e.matmul(
                        PS[0:1, t4 * 512:(t4 + 1) * 512], ones_bf[:],
                        xbf_slot(p)[:, t4 * 512:(t4 + 1) * 512],
                        start=(j == 0), stop=(j == JD - 1)))
                for t4 in range(4):
                    E("tensor", lambda e, j=j, p=p, t4=t4: e.matmul(
                        PS[0:1, 2048 + t4 * 512:2048 + (t4 + 1) * 512],
                        ones_bf[:],
                        xsq_slot(p)[:, t4 * 512:(t4 + 1) * 512],
                        start=(j == 0), stop=(j == JD - 1)),
                        sem=(spe if t4 == 3 else None))
                pe_done[("1a", j)] = spe.n

        mark(f"s{s}:stats")
        W("vector", spe, pe_done[("1a", JD - 1)])
        E("vector", lambda e: e.tensor_scalar(
            m_row, PS[0:1, 0:2048], 1.0 / D, None, ALU.mult), sem=sdve)
        E("vector", lambda e: e.tensor_tensor(
            nminvs_row, m_row, m_row, ALU.mult), sem=sdve)
        E("vector", lambda e: e.scalar_tensor_tensor(
            PS[0:1, 0:2048], PS[0:1, 2048:4096], 1.0 / D, nminvs_row,
            ALU.mult, ALU.subtract), sem=sdve)
        dve_var = sdve.n
        W("scalar", sdve, dve_var)
        W("scalar", sgp, memset_done)
        E("scalar", lambda e: e.activation(
            invs_row, PS[0:1, 0:2048], AF.Sqrt, bias=eps_c[:]), sem=sact)
        act_sd = sact.n
        W("vector", sact, act_sd)

        def _recip(e):
            with nc.allow_low_precision(reason="bf16 normalizer row"):
                return e.reciprocal(invs_row, invs_row)
        E("vector", _recip, sem=sdve)
        E("vector", lambda e: e.scalar_tensor_tensor(
            nminvs_row, m_row, -1.0, invs_row, ALU.mult, ALU.mult), sem=sdve)
        dve_rows = sdve.n
        W("tensor", sdve, dve_rows)
        for t4 in range(4):
            E("tensor", lambda e, t4=t4: e.matmul(
                bank(t4), onesr_bf[:], invs_row[:, t4 * 512:(t4 + 1) * 512],
                start=True, stop=True))
        for t4 in range(4):
            E("tensor", lambda e, t4=t4: e.matmul(
                bank(4 + t4), onesr_bf[:],
                nminvs_row[:, t4 * 512:(t4 + 1) * 512],
                start=True, stop=True), sem=(spe if t4 == 3 else None))
        pe_rep = spe.n
        W("vector", spe, pe_rep)
        E("vector", lambda e: e.tensor_copy(sA[:], PS[:, 0:2048]), sem=sdve)
        E("vector", lambda e: e.tensor_copy(sII[:], PS[:, 2048:4096]),
          sem=sdve)
        dve_rep = sdve.n

        if s == 0:
            conv_gate("scalar", 1)
        load_slab(("wif", s, 0), wifb[0:128, :])
        load_slab(("wif", s, 1), wifb[128:256, :])

        mark(f"s{s}:Ph1b")
        xn_done = {}
        for jp in range(JD // 2):
            pp_ = jp % 2
            if s == 0:
                conv_emit(1)
            if jp <= 1:
                W("sync", sact, act_done[("1a", JD - 1)])
                W("sync", sgp, gp_done[("1a", JD - 1)])
            if jp >= 2:
                sm, cn = xn_done[2 * jp - 3]
                W("sync", sm, cn)
                sm, cn = xn_done[2 * jp - 4]
                W("sync", sm, cn)
            E("sync", lambda e, jp=jp, xrow=xrow: e.dma_start(
                pbuf_full(jp), xt_pair[(xrow // 128) + 2 * jp:
                                       (xrow // 128) + 2 * jp + 2].rearrange(
                    "r p t -> p r t")), sem=sx[pp_], k=16)
            for jj in range(2):
                j = 2 * jp + jj
                xin = pbuf(jp, jj)
                qn = "vector" if jj == 0 else "gpsimd"
                qsem = sdve if jj == 0 else sgp
                W(qn, sx[pp_])
                if j <= 1:
                    W(qn, sdve, dve_rep)
                E(qn, lambda e, xin=xin: e.tensor_tensor(
                    xin[:], xin[:], sA[:], ALU.mult))
                E(qn, lambda e, j=j, xin=xin: e.tensor_tensor(
                    xn_j(j), xin[:], sII[:], ALU.add), sem=qsem)
                xn_done[j] = (qsem, qsem.n)
                W("scalar", qsem, qsem.n)
                E("scalar", lambda e, j=j: e.dma_start(
                    xnd[j * 128:(j + 1) * 128, :], xn_j(j)), sem=sxn, k=16)
        xn_stores = sxn.n

        mark(f"s{s}:Ph2")
        W("gpsimd", sdve, dve_rep)
        E("gpsimd", lambda e: e.memset(sACC[:], 0.0), sem=sgp)
        acc_zero = sgp.n

        if s == 0:
            a_emit_load(0, xn_done)
            a_emit_load(1, xn_done)
        evac_act = {}
        evac_dve = {}
        scan_done = {}
        hs_store = {}
        for k in range(KE):
            iw, iwc, pw = slab_ld[("wif", s, k)]
            fw = iw + 1
            if s == 0:
                conv_emit(2)
            for qt in range(4):
                hof = qt * 512
                g3 = (k * 4 + qt) % 3
                b_i, b_f = 2 * g3, 2 * g3 + 1
                if k == 0 and qt == 0:
                    W("tensor", sdve, dve_rep)
                mprev = k * 4 + qt - 3
                if mprev >= 0:
                    kp, qp = divmod(mprev, 4)
                    W("tensor", sdve, evac_dve[(kp, qp)])
                    W("tensor", sact, evac_act[(kp, qp)])
                if qt == 0:
                    W("tensor", sw[pw], iwc)
                for j in range(JD):
                    if k == 0 and qt == 0:
                        W("tensor", xn_done[j][0], xn_done[j][1])
                    E("tensor", lambda e, j=j, b_i=b_i, iw=iw, hof=hof:
                      e.matmul(
                        bank(b_i), wslab(iw)[:, j * 128:(j + 1) * 128],
                        xn_j(j)[:, hof:hof + 512],
                        start=(j == 0), stop=(j == JD - 1)))
                    E("tensor", lambda e, j=j, b_f=b_f, fw=fw, hof=hof:
                      e.matmul(
                        bank(b_f), wslab(fw)[:, j * 128:(j + 1) * 128],
                        xn_j(j)[:, hof:hof + 512],
                        start=(j == 0), stop=(j == JD - 1)),
                        sem=(spe if j == JD - 1 else None))
                pe_done[("2", k, qt)] = spe.n
                if s == 0:
                    nq_ = (k * 4 + qt) * 2
                    a_emit_round(nq_, xn_done)
                    a_emit_round(nq_ + 1, xn_done)
                W("scalar", spe, pe_done[("2", k, qt)])
                if qt == 0 and k > 0:
                    W("scalar", sdve, scan_done[k - 1])
                E("scalar", lambda e, b_i=b_i, hof=hof: e.activation(
                    xinB[:, hof:hof + 512], bank(b_i), AF.Sigmoid))
                E("scalar", lambda e, b_f=b_f, hof=hof: e.activation(
                    sA[:, hof:hof + 512], bank(b_f), AF.Sigmoid), sem=sact)
                evac_act[(k, qt)] = sact.n
                W("vector", sact, evac_act[(k, qt)])
                E("vector", lambda e, hof=hof: e.tensor_scalar(
                    xinA[:, hof:hof + 512], sA[:, hof:hof + 512], -1.0, 1.0,
                    ALU.mult, ALU.add))
                E("vector", lambda e, b_i=b_i, hof=hof: e.tensor_tensor(
                    xinB[:, hof:hof + 512], bank(b_i),
                    xinB[:, hof:hof + 512], ALU.mult))
                E("vector", lambda e, hof=hof: e.tensor_tensor(
                    sII[:, hof:hof + 512], xinB[:, hof:hof + 512],
                    xinA[:, hof:hof + 512], ALU.mult), sem=sdve)
                evac_dve[(k, qt)] = sdve.n
            slab_consumed[iw] = pe_done[("2", k, 3)]
            slab_consumed[fw] = pe_done[("2", k, 3)]
            if k + 2 < KE:
                if s == 0:
                    conv_gate("scalar", 2 * ((k + 2) // 2) + 1)
                load_slab(("wif", s, k + 2),
                          wifb[(k + 2) * 128:(k + 3) * 128, :])
            if k >= 2:
                W("vector", sgp, gp_done[("2", k - 2)])
                W("vector", sst, hs_store[k - 2])
            E("vector", lambda e, k=k: e.tensor_tensor_scan(
                hsb(k), sA[:], sII[:], 0.0, ALU.mult, ALU.add), sem=sdve)
            scan_done[k] = sdve.n
            W("sync", sdve, scan_done[k])
            E("sync", lambda e, k=k: e.dma_start(
                hs[k * 128:(k + 1) * 128, :], hsb(k)), sem=sst, k=16)
            hs_store[k] = sst.n
            W("gpsimd", sdve, scan_done[k])
            if k == 0:
                W("gpsimd", sgp, acc_zero)
            E("gpsimd", lambda e, k=k: e.tensor_tensor(
                sH[:], hsb(k), hsb(k), ALU.mult))
            E("gpsimd", lambda e: e.tensor_tensor(
                sACC[:], sACC[:], sH[:], ALU.add), sem=sgp)
            gp_done[("2", k)] = sgp.n

        mark(f"s{s}:rms")
        W("tensor", sgp, gp_done[("2", KE - 1)])
        W("tensor", sdve, evac_dve[(KE - 1, 3)])
        W("tensor", sact, evac_act[(KE - 1, 3)])
        for t4 in range(4):
            E("tensor", lambda e, t4=t4: e.matmul(
                PS[0:1, t4 * 512:(t4 + 1) * 512], ones_f[:],
                sACC[:, t4 * 512:(t4 + 1) * 512],
                start=True, stop=True), sem=(spe if t4 == 3 else None))
        pe_ssq = spe.n
        W("scalar", spe, pe_ssq)
        E("scalar", lambda e: e.activation(
            invs_row, PS[0:1, 0:2048], AF.Sqrt, bias=eps_c[:],
            scale=1.0 / D), sem=sact)
        act_sd2 = sact.n
        W("vector", sact, act_sd2)

        def _recip2(e):
            with nc.allow_low_precision(reason="bf16 normalizer row"):
                return e.reciprocal(invs_row, invs_row)
        E("vector", _recip2, sem=sdve)
        dve_rstd = sdve.n
        W("tensor", sdve, dve_rstd)
        W("tensor", sdve, evac_dve[(KE - 1, 3)])
        W("tensor", sact, evac_act[(KE - 1, 3)])
        if s == 0:
            W("tensor", sact, aw_evac[127])
        for t4 in range(4):
            E("tensor", lambda e, t4=t4: e.matmul(
                bank(4 + t4), onesr_bf[:],
                invs_row[:, t4 * 512:(t4 + 1) * 512],
                start=True, stop=True), sem=(spe if t4 == 3 else None))
        pe_rstd = spe.n
        W("vector", spe, pe_rstd)
        W("vector", sgp, gp_done[("2", KE - 1)])
        E("vector", lambda e: e.tensor_copy(sACC[:], PS[:, 2048:4096]),
          sem=sdve)
        dve_rstd_rep = sdve.n

        mark(f"s{s}:Ph3")
        if s == 0:
            conv_gate("scalar", CONV_WG0 + 0)
        load_slab(("wg", s, 0),
                  wgb[0:256, :].rearrange("(n p) d -> p n d", p=128))
        og_store = {}
        hsin_ld = {}
        for k in range(2):
            W("sync", sst, hs_store[KE - 1])
            W("sync", sgp, gp_done[("2", KE - 1)])
            E("sync", lambda e, k=k: e.dma_start(
                hsb(k), hs[k * 128:(k + 1) * 128, :]), sem=sq[0], k=16)
            hsin_ld[k] = sq[0].n
        for k in range(KE):
            if s == 0:
                conv_emit(3)
            gw0, gwc, pg = slab_ld[("wg", s, k - k % 2)]
            gw = gw0 + k % 2
            base = 0 if k % 2 == 0 else 4
            if k % 2 == 0:
                W("tensor", sw[pg], gwc)
            if k == 0:
                W("tensor", sact, act_sd2)
            if k == 1:
                W("tensor", sdve, dve_rstd_rep)
            if k >= 2:
                W("tensor", sdve, g_dve[(s, k - 2)])
            for j in range(JD):
                for t4 in range(4):
                    E("tensor", lambda e, j=j, base=base, gw=gw, t4=t4:
                      e.matmul(
                        bank(base + t4),
                        wslab(gw)[:, j * 128:(j + 1) * 128],
                        xn_j(j)[:, t4 * 512:(t4 + 1) * 512],
                        start=(j == 0), stop=(j == JD - 1)),
                        sem=(spe if (j == JD - 1 and t4 == 3) else None))
            pe_done[("3", k)] = spe.n
            slab_consumed[gw] = pe_done[("3", k)]
            if k % 2 == 0 and k + 2 < KE:
                if s == 0:
                    conv_gate("scalar", CONV_WG0 + (k + 2) // 2)
                load_slab(("wg", s, k + 2),
                          wgb[(k + 2) * 128:(k + 4) * 128, :].rearrange(
                              "(n p) d -> p n d", p=128))
            W("scalar", spe, pe_done[("3", k)])
            E("scalar", lambda e, base=base: e.activation(
                xinB[:], PS[:, base * 512:(base + 4) * 512], AF.Sigmoid),
                sem=sact)
            act_done[("3", k)] = sact.n
            W("vector", sact, act_done[("3", k)])
            E("vector", lambda e, base=base: e.tensor_tensor(
                xinA[:], PS[:, base * 512:(base + 4) * 512], xinB[:],
                ALU.mult))
            E("vector", lambda e, k=k: e.scalar_tensor_tensor(
                xinB[:], xinA[:], gnw_sb[:, k:k + 1], sACC[:], ALU.mult,
                ALU.mult))
            W("vector", sq[0], hsin_ld[k])
            if k >= 2:
                W("vector", sst, og_store[k - 2])
            E("vector", lambda e, k=k: e.tensor_tensor(
                ogo(k), xinB[:], hsb(k), ALU.mult), sem=sdve)
            g_dve[(s, k)] = sdve.n
            W("sync", sdve, g_dve[(s, k)])
            E("sync", lambda e, k=k: e.dma_start(
                ogs[k * 128:(k + 1) * 128, :], ogo(k)), sem=sst, k=16)
            og_store[k] = sst.n
            if k + 2 < KE:
                E("sync", lambda e, k=k: e.dma_start(
                    hsb(k + 2), hs[(k + 2) * 128:(k + 3) * 128, :]),
                    sem=sq[0], k=16)
                hsin_ld[k + 2] = sq[0].n
        og_all = sst.n
        if s == 0:
            b_pe_last, b_rl, b_evac_last, b_st127 = b_emit_block(
                g_dve[(s, KE - 1)], act_done[("3", KE - 1)])

        ring_pe = {}
        ring_i = [0]
        q_xnq_done = {}
        hid_dve = {}
        out_dve = {}
        outst = {}
        pe_4b_last = {}
        xnq_lds = {}
        for q in range(NQ):
            pq = q % 2
            if q == 0:
                W("sync", sst, og_all)
                if s == 0:
                    W("sync", sst, b_st127)
                W("sync", sxn, xn_stores)
                E("sync", lambda e: e.dma_start(
                    xnq_slot(), xnd_v[:, :, 0:TQ]), sem=sq[0], k=16)
                xnq_lds[0] = sq[0].n
            xnq_ld = xnq_lds[q]
            if q > 0:
                W("sync", spe, pe_4b_last[q - 1])
            if s == 0 and q == 0:
                W("sync", spe, b_pe_last)
            W("sync", sst, og_all)
            E("sync", lambda e, q=q: e.dma_start(
                ogq_slot(), ogs_v[:, :, q * TQ:(q + 1) * TQ]),
                sem=sq[1], k=16)
            ogq_ld = sq[1].n

            mark(f"s{s}:Ph4a{q}")
            if s == 0 and q == 0:
                conv_gate("scalar", CONV_W13 + 1)
            if s == 0 and q == 1:
                W("scalar", sgst, sgst.n)
            load_slab(("w13", s, q, 0), w13b[0:128, :])
            load_slab(("w13", s, q, 1), w13b[128:256, :])
            for hc in range(KH):
                ph = hc % 2
                b_ug, b_ul = (0, 1) if ph == 0 else (2, 3)
                i1, c1, p1 = slab_ld[("w13", s, q, hc)]
                i3 = i1 + 1
                W("tensor", sw[p1], c1)
                if hc == 0:
                    W("tensor", sq[0], xnq_ld)
                    if q == 0:
                        W("tensor", sdve, g_dve[(s, KE - 1)])
                        if s == 0:
                            W("tensor", b_evac_last[0], b_evac_last[1])
                if (q, hc - 2) in hid_dve:
                    W("tensor", sdve, hid_dve[(q, hc - 2)])
                for j in range(JD):
                    E("tensor", lambda e, j=j, b_ug=b_ug, i1=i1: e.matmul(
                        bank(b_ug), wslab(i1)[:, j * 128:(j + 1) * 128],
                        xnq_slot()[:, j * TQ:(j + 1) * TQ],
                        start=(j == 0), stop=(j == JD - 1)))
                    E("tensor", lambda e, j=j, b_ul=b_ul, i3=i3: e.matmul(
                        bank(b_ul), wslab(i3)[:, j * 128:(j + 1) * 128],
                        xnq_slot()[:, j * TQ:(j + 1) * TQ],
                        start=(j == 0), stop=(j == JD - 1)),
                        sem=(spe if j == JD - 1 else None))
                pe_done[("4a", q, hc)] = spe.n
                slab_consumed[i1] = spe.n
                slab_consumed[i3] = spe.n
                if hc + 2 < KH:
                    if s == 0 and q == 0:
                        conv_gate("scalar",
                                  CONV_W13 + 2 * ((hc + 2) // 2) + 1)
                    load_slab(("w13", s, q, hc + 2),
                              w13b[(hc + 2) * 128:(hc + 3) * 128, :])
                W("scalar", spe, pe_done[("4a", q, hc)])
                E("scalar", lambda e, b_ug=b_ug, ph=ph: e.activation(
                    sg_slot(ph), bank(b_ug), AF.Sigmoid), sem=sact)
                act_done[("4a", q, hc)] = sact.n
                W("vector", sact, act_done[("4a", q, hc)])
                E("vector", lambda e, b_ug=b_ug, ph=ph: e.tensor_tensor(
                    tmp_slot(ph), bank(b_ug), sg_slot(ph), ALU.mult))
                E("vector", lambda e, b_ul=b_ul, ph=ph, hc=hc:
                  e.tensor_tensor(
                    hidq_h(hc), tmp_slot(ph), bank(b_ul), ALU.mult),
                    sem=sdve)
                hid_dve[(q, hc)] = sdve.n
            q_xnq_done[q] = pe_done[("4a", q, KH - 1)]

            mark(f"s{s}:Ph4b{q}")
            for dc in range(JD):
                pd = dc % 2
                b_out = 6 + pd
                key = (q, dc - 2) if dc >= 2 else (q - 1, JD - 2 + dc)
                if key in outst:
                    W("gpsimd", sout, outst[key])
                if q == 0 and dc < 2:
                    W("gpsimd", sdve, g_dve[(s, KE - 1)])
                E("gpsimd", lambda e, dc=dc, q=q, pd=pd, xrow=xrow:
                  e.dma_start(
                    outb_slot(pd),
                    xt[xrow + dc * 128:xrow + (dc + 1) * 128,
                       q * TQ:(q + 1) * TQ]), sem=sx[pd], k=16)
                xres_ld = sx[pd].n
                if dc == 0:
                    W("tensor", sdve, hid_dve[(q, KH - 1)])
                    W("tensor", sq[1], ogq_ld)
                    if q == 0:
                        W("sync", sst, og_all)
                    if q + 1 < NQ:
                        W("sync", spe, q_xnq_done[q])
                        W("sync", sxn, xn_stores)
                        E("sync", lambda e, q=q: e.dma_start(
                            xnq_slot(),
                            xnd_v[:, :, (q + 1) * TQ:(q + 2) * TQ]),
                            sem=sq[0], k=16)
                        xnq_lds[q + 1] = sq[0].n
                if (q, dc - 2) in out_dve:
                    W("tensor", sdve, out_dve[(q, dc - 2)])
                elif (q - 1, JD - 2 + dc) in out_dve:
                    W("tensor", sdve, out_dve[(q - 1, JD - 2 + dc)])
                for gi in range(3):
                    ri = ring_i[0]
                    ring_i[0] += 1
                    if ri - 4 in ring_pe:
                        W("sync", spe, ring_pe[ri - 4])
                    if gi < 2:
                        if s == 0 and q == 0 and gi == 0:
                            W("sync", srl, b_rl[dc])
                            if dc == 0:
                                W("sync", sgst, sgst.n)
                        E("sync", lambda e, dc=dc, gi=gi, ri=ri: e.dma_start(
                            ring_slot(ri),
                            w2p[dc * 128:(dc + 1) * 128,
                                gi * 4096:(gi + 1) * 4096]),
                            sem=sr[ri % 2], k=16)
                    else:
                        if s == 0 and q == 0:
                            conv_gate("sync", CONV_WO + dc // 2)
                        E("sync", lambda e, dc=dc, ri=ri: e.dma_start(
                            ring_slot(ri)[:, 0:2048],
                            wob[dc * 128:(dc + 1) * 128, :]),
                            sem=sr[ri % 2], k=16)
                    rc = sr[ri % 2].n
                    W("tensor", sr[ri % 2], rc)
                    nt = 32 if gi < 2 else 16
                    for ti in range(nt):
                        if gi < 2:
                            hc = gi * 32 + ti
                            E("tensor", lambda e, b_out=b_out, ri=ri, ti=ti,
                              hc=hc: e.matmul(
                                bank(b_out),
                                ring_slot(ri)[:, ti * 128:(ti + 1) * 128],
                                hidq_h(hc),
                                start=(hc == 0), stop=False),
                                sem=(spe if ti == nt - 1 else None))
                        else:
                            E("tensor", lambda e, b_out=b_out, ri=ri, ti=ti:
                              e.matmul(
                                bank(b_out),
                                ring_slot(ri)[:, ti * 128:(ti + 1) * 128],
                                ogq_slot()[:, ti * TQ:(ti + 1) * TQ],
                                start=False, stop=(ti == 15)),
                                sem=(spe if ti == 15 else None))
                    ring_pe[ri] = spe.n
                pe_out = spe.n
                pe_4b_last[q] = pe_out
                W("vector", spe, pe_out)
                W("vector", sx[pd], xres_ld)
                E("vector", lambda e, b_out=b_out, pd=pd: e.tensor_tensor(
                    outb_slot(pd), bank(b_out), outb_slot(pd), ALU.add),
                    sem=sdve)
                out_dve[(q, dc)] = sdve.n
                W("gpsimd", sdve, out_dve[(q, dc)])
                E("gpsimd", lambda e, dc=dc, q=q, pd=pd, xrow=xrow:
                  e.dma_start(
                    out[xrow + dc * 128:xrow + (dc + 1) * 128,
                        q * TQ:(q + 1) * TQ], outb_slot(pd)),
                    sem=sout, k=16)
                outst[(q, dc)] = sout.n

        for qn in PROG:
            W(qn, spe, spe.n)
            W(qn, sact, sact.n)
            W(qn, sdve, sdve.n)
            W(qn, sgp, sgp.n)
            W(qn, sout, sout.n)

    W("sync", sout, sout.n)

    with nc.Block() as block:
        @block.sync
        def _(eng):
            for f in PROG["sync"]:
                f(eng)

        @block.gpsimd
        def _(eng):
            for f in PROG["gpsimd"]:
                f(eng)

        @block.scalar
        def _(eng):
            for f in PROG["scalar"]:
                f(eng)

        @block.vector
        def _(eng):
            for f in PROG["vector"]:
                f(eng)

        @block.tensor
        def _(eng):
            for f in PROG["tensor"]:
                f(eng)

    return nc


def kernel(L0_attn_hidden_states, L0_attn_Wi, L0_attn_Wf, L0_attn_Wg,
           L0_attn_Wo, L0_attn_g_norm_weight, L0_ffn_W1, L0_ffn_W2,
           L0_ffn_W3):
    x = np.asarray(L0_attn_hidden_states, dtype=np.float32)
    Wi = np.asarray(L0_attn_Wi, dtype=np.float32)
    Wf = np.asarray(L0_attn_Wf, dtype=np.float32)
    Wg = np.asarray(L0_attn_Wg, dtype=np.float32)
    Wo = np.asarray(L0_attn_Wo, dtype=np.float32)
    gnwv = np.asarray(L0_attn_g_norm_weight, dtype=np.float32)
    W1 = np.asarray(L0_ffn_W1, dtype=np.float32)
    W2 = np.asarray(L0_ffn_W2, dtype=np.float32)
    W3 = np.asarray(L0_ffn_W3, dtype=np.float32)

    xt_all = np.ascontiguousarray(x.transpose(0, 2, 1))

    def tile_de(W):
        return np.ascontiguousarray(
            W.reshape(16, 128, 16, 128).transpose(2, 1, 0, 3).reshape(
                2048, 2048))

    def tile_hd(W):
        return np.ascontiguousarray(
            W.reshape(64, 128, 16, 128).transpose(0, 3, 2, 1).reshape(
                8192, 2048))

    wir = tile_de(Wi)
    wfr = tile_de(Wf)
    wgr = tile_de(Wg)
    wor = tile_de(Wo)
    w1r = tile_hd(W1)
    w3r = tile_hd(W3)
    w2tt = np.ascontiguousarray(W2.T)
    h128v = np.ascontiguousarray(
        (_hadamard(128) * (2.0 ** -7)).astype(ml_dtypes.bfloat16))
    h64v = np.ascontiguousarray(
        (_hadamard(64) * np.sqrt(2.0)).astype(ml_dtypes.bfloat16))
    gnw2 = np.ascontiguousarray(gnwv.reshape(2048, 1))

    nc = build_nc()
    in_maps = []
    for c in range(8):
        in_maps.append({
            "xt": np.ascontiguousarray(
                xt_all[2 * c:2 * c + 2].reshape(NS * D, T)),
            "wir": wir, "wfr": wfr, "wgr": wgr, "wor": wor,
            "w1r": w1r, "w3r": w3r, "w2t": w2tt,
            "gnw": gnw2, "h128": h128v, "h64": h64v,
        })
    res = run_bass_kernel_spmd(nc, in_maps, core_ids=list(range(8)))
    outs = [res.results[c]["out"].reshape(NS, D, T) for c in range(8)]
    full = np.concatenate(outs, axis=0)
    return np.ascontiguousarray(full.transpose(0, 2, 1))
